# version 03 (final)
# baseline (speedup 1.0000x reference)
import numpy as np
import ml_dtypes
from contextlib import ExitStack

import concourse.bass as bass
import concourse.mybir as mybir
from concourse.bass_utils import run_bass_kernel_spmd

F32 = mybir.dt.float32
BF16 = mybir.dt.bfloat16
AX = mybir.AxisListType
ALU = mybir.AluOpType
AF = mybir.ActivationFunctionType

H, P, G, N, CS = 128, 64, 8, 128, 128
DIM, INTER = 4096, 8192
CONV_DIM = INTER + 2 * G * N
NPROJ = INTER + CONV_DIM + H
EPS = 1e-5
TOK = 1024
CN = TOK // CS
KD = DIM // 128
OB_GATE0, OB_X0, OB_B0, OB_C0, OB_DT = 0, 64, 128, 136, 144
HBLK = 64

_TAPS = []
_SUB = 99
_PHASES = 99

ENGS = ("tensor", "vector", "scalar", "gpsimd", "sync")


class _Op:
    __slots__ = ("eng", "fn", "deps", "need_inc", "is_dma", "sem", "val",
                 "amt", "pre_drain", "sem_override")

    def __init__(self, eng, fn, deps, is_dma):
        self.eng, self.fn, self.deps, self.is_dma = eng, fn, deps, is_dma
        self.need_inc = is_dma
        self.sem = self.val = None
        self.amt = 16 if is_dma else 1
        self.sem_override = None
        self.pre_drain = (not is_dma) and eng in ("vector", "scalar", "gpsimd")


class Sched:
    def __init__(self):
        self.per_eng = {e: [] for e in ENGS}

    def op(self, eng, fn, deps=(), dma=False, hard=(), drain=None):
        dd = []
        stack = list(deps)
        while stack:
            d = stack.pop()
            if d is None:
                continue
            if isinstance(d, (list, tuple)):
                stack.extend(d)
            else:
                dd.append(d)
        kept = []
        for d in dd:
            if d.eng == eng and not d.is_dma:
                continue
            d.need_inc = True
            kept.append(d)
        for d in hard:
            if d is None:
                continue
            d.need_inc = True
            kept.append(d)
        o = _Op(eng, fn, kept, dma)
        if drain is not None:
            o.pre_drain = drain and not dma
        self.per_eng[eng].append(o)
        return o

    def cc(self, fn, deps=()):
        o = self.op("gpsimd", fn, deps=deps, dma=True)
        o.amt = 1
        o.sem_override = "dC"
        return o

    def fence(self):
        toks = []
        for e in ENGS:
            if self.per_eng[e]:
                o = self.per_eng[e][-1]
                o.need_inc = True
                toks.append(o)
        return toks

    def dma(self, q, out, in_, deps=(), **kw):
        return self.op(q, lambda e: e.dma_start(out=out, in_=in_, **kw),
                       deps=deps, dma=True)

    def dmaT(self, q, out, in_, deps=()):
        return self.op(q, lambda e: e.dma_start_transpose(out, in_),
                       deps=deps, dma=True)

    def finalize_and_replay(self, nc):
        semname = {("tensor", False): "tP", ("vector", False): "tV",
                   ("scalar", False): "tA", ("gpsimd", False): "tG",
                   ("sync", False): "tS",
                   ("gpsimd", True): "dG", ("sync", True): "dS",
                   ("scalar", True): "dA", ("vector", True): "dV",
                   ("tensor", True): "dT"}
        counts = {}
        nops = 0
        for e in ENGS:
            for o in self.per_eng[e]:
                nops += 1
                if o.need_inc:
                    o.sem = o.sem_override or semname[(e, o.is_dma)]
                    counts[o.sem] = counts.get(o.sem, 0) + o.amt
                    o.val = counts[o.sem]
        with ExitStack() as stk:
            sems = {s: stk.enter_context(nc.semaphore(s)) for s in counts}
            blk = stk.enter_context(nc.Block())
            for e in ENGS:
                items = self.per_eng[e]

                def mk(items=items):
                    def body(h):
                        hi = {}
                        prev_compute = False
                        for o in items:
                            if o.pre_drain and prev_compute:
                                h.drain()
                            for d in o.deps:
                                if d.val is not None and d.val > hi.get(d.sem, 0):
                                    h.wait_ge(sems[d.sem], d.val)
                                    hi[d.sem] = d.val
                            inst = o.fn(h)
                            prev_compute = not o.is_dma
                            if o.need_inc:
                                assert inst is not None
                                inst.then_inc(sems[o.sem], o.amt)
                    return body
                getattr(blk, e)(mk())


class Arena:
    def __init__(self, t, ncols, name):
        self.t, self.ncols, self.top, self.name = t, ncols, 0, name
        self.marks = []
        self.peak = 0

    def al(self, cols):
        alloc = (cols + 15) & ~15
        off = self.top
        self.top += alloc
        self.peak = max(self.peak, self.top)
        assert self.top <= self.ncols, \
            f"{self.name} overflow {self.top}>{self.ncols}"
        return self.t[:, off:off + cols]

    def push(self):
        self.marks.append(self.top)

    def pop(self):
        self.top = self.marks.pop()


def build_core_program(taps=(), phases=99, sub=None):
    sub = _SUB if sub is None else sub
    nc = bass.Bass("TRN2", target_bir_lowering=False)

    dx = nc.dram_tensor("x", [TOK, DIM], F32, kind="ExternalInput").ap()
    dxh = nc.dram_tensor("xh", [3, DIM], F32, kind="ExternalInput").ap()
    dwpT = nc.dram_tensor("wpT", [DIM, NPROJ], F32, kind="ExternalInput").ap()
    dwoT = nc.dram_tensor("woT", [INTER, DIM], F32, kind="ExternalInput").ap()
    dwbT = nc.dram_tensor("wbT", [DIM, DIM], F32, kind="ExternalInput").ap()
    dwrT = nc.dram_tensor("wrT", [DIM, 8], F32, kind="ExternalInput").ap()
    dlaT = nc.dram_tensor("laT", [DIM, 128], F32, kind="ExternalInput").ap()
    dlbF = nc.dram_tensor("lbF", [128, DIM], F32, kind="ExternalInput").ap()
    dcw = nc.dram_tensor("cw", [80, 128, 4], F32, kind="ExternalInput").ap()
    dcbc = nc.dram_tensor("cbc", [128, 80], F32, kind="ExternalInput").ap()
    ddbc = nc.dram_tensor("dbc", [128, 1], F32, kind="ExternalInput").ap()
    dalc = nc.dram_tensor("alc", [128, 1], F32, kind="ExternalInput").ap()
    ddc = nc.dram_tensor("dc", [128, 1], F32, kind="ExternalInput").ap()
    dnwc = nc.dram_tensor("nwc", [128, 64], F32, kind="ExternalInput").ap()
    dbbc = nc.dram_tensor("bbc", [128, 32], F32, kind="ExternalInput").ap()
    dbrc = nc.dram_tensor("brc", [8, 1], F32, kind="ExternalInput").ap()
    didb = nc.dram_tensor("idb", [128, 128], BF16, kind="ExternalInput").ap()
    did32 = nc.dram_tensor("id32", [128, 128], F32, kind="ExternalInput").ap()
    dmut = nc.dram_tensor("mut", [128, 128], F32, kind="ExternalInput").ap()
    dinvk = nc.dram_tensor("invk", [128, 8], F32, kind="ExternalInput").ap()
    dccm = nc.dram_tensor("ccm", [128, 8], F32, kind="ExternalInput").ap()
    dout = nc.dram_tensor("out", [TOK, DIM], F32, kind="ExternalOutput").ap()

    sgD = nc.dram_tensor("sgD", [HBLK, CN, 128, 128], BF16).ap()
    xfD = nc.dram_tensor("xfD", [HBLK, CN, 128, 128], BF16).ap()
    zD = nc.dram_tensor("zD", [CN, HBLK, 128, 128], BF16).ap()
    szD = nc.dram_tensor("szD", [CN, HBLK, 128, 128], BF16).ap()
    gD = nc.dram_tensor("gD", [32, 128, TOK], BF16).ap()
    gmtD = nc.dram_tensor("gmtD", [CN, G, 128, 128], BF16).ap()
    cfmD = nc.dram_tensor("cfmD", [CN, G, 128, 128], BF16).ap()
    btmD = nc.dram_tensor("btmD", [G, CN, 128, 128], BF16).ap()
    ccSi = nc.dram_tensor("ccSi", [128, N * P], BF16).ap()
    ccSo = nc.dram_tensor("ccSo", [512, N * P], BF16).ap()
    ccdi = nc.dram_tensor("ccdi", [128, 1], F32).ap()
    ccdo = nc.dram_tensor("ccdo", [512, 1], F32).ap()

    tap_shapes = {"xn0": ([128, DIM], F32), "dtfm": ([128, TOK], F32),
                  "csT": ([128, TOK], F32), "expcsT": ([128, TOK], F32),
                  "dtT": ([128, TOK], F32), "dtdecT": ([128, TOK], F32),
                  "lam": ([128, CN], F32), "dcol": ([128, 1], F32),
                  "decall": ([128, TOK], F32), "cfm": ([128, G * TOK], BF16),
                  "gmt0": ([128, 128], BF16), "btm0": ([128, 128], BF16),
                  "srun": ([128, N * P], F32), "sincore": ([128, N * P], F32),
                  "ssq": ([128, CN], F32), "rstd": ([128, CN], F32),
                  "hff": ([128, TOK], F32), "wT0": ([128, 8], F32),
                  "logit": ([8, TOK], F32), "zc0": ([128, HBLK * 128], BF16),
                  "lasb": ([128, KD * 128], BF16)}
    tap_outs = {t: nc.dram_tensor("tap_" + t, tap_shapes[t][0],
                                  tap_shapes[t][1], kind="ExternalOutput").ap()
                for t in taps}

    S = Sched()
    stk = ExitStack()
    A32 = Arena(stk.enter_context(nc.sbuf_tensor("AR32", [128, 24100], F32)),
                24100, "A32")
    A16 = Arena(stk.enter_context(nc.sbuf_tensor("AR16", [128, 54200], BF16)),
                54200, "A16")
    pb = [stk.enter_context(nc.psum_tensor(f"pb{i}", [128, 512], F32))
          for i in range(8)]
    pX = pb[6]
    pXb = pX.ap().bitcast(BF16)
    pS_ = pb[7]
    pSb = pS_.ap().bitcast(BF16)

    id32 = A32.al(128); mut = A32.al(128); invk = A32.al(8); ccm = A32.al(8)
    cbc = A32.al(80); dbc = A32.al(1); alc = A32.al(1); acol = A32.al(1)
    dc_ = A32.al(1); nwc = A32.al(64); bbc = A32.al(32)
    brc_t = stk.enter_context(nc.sbuf_tensor("brcs", [8, 16], F32))
    brc = brc_t[:, 0:1]
    ones = A32.al(1027)
    epsc = A32.al(1)
    cwb = A32.al(8)
    csT = A32.al(TOK); expcsT = A32.al(TOK); dtT = A32.al(TOK)
    lamc = A32.al(CN); dcolc = A32.al(1); dful = A32.al(128)
    ssq_sb = A32.al(CN); rstd_sb = A32.al(CN)
    tcum = A32.al(CN); tcum2 = A32.al(CN); etc_ = A32.al(CN)
    bigr = A32.al(N * P)
    srun = bigr
    idb = A16.al(128)
    cshiT = A16.al(TOK); csloT = A16.al(TOK)

    t_consts = []
    for dst, src in ((idb, didb), (id32, did32), (mut, dmut), (invk, dinvk),
                     (ccm, dccm), (cbc, dcbc), (dbc, ddbc), (alc, dalc),
                     (dc_, ddc), (nwc, dnwc), (bbc, dbbc), (brc, dbrc)):
        t_consts.append(S.dma("sync", dst, src))
    tc_idb, tc_id32, tc_mut, tc_invk, tc_ccm, tc_cbc, tc_dbc, tc_alc, \
        tc_dc, tc_nwc, tc_bbc, tc_brc = t_consts
    t_ones = S.op("vector", lambda e: e.memset(ones, 1.0))
    t_eps = S.op("vector", lambda e: e.memset(epsc, EPS))


    def _finish_early(deps):
        toks = []
        for c in range(CN):
            toks.append(S.dma("sync", dout[c * 128:(c + 1) * 128, :],
                              bigr[:, 0:DIM],
                              deps=deps + [t_ones] + toks[-1:]))
        S.op("sync", lambda e: e.nop(), deps=toks)
        S.finalize_and_replay(nc)
        stk.close()
        return nc

    px_guard = {}
    bkg = {6: None, 7: None}
    PBF = {6: pX, 7: pS_}
    PBB = {6: pXb, 7: pSb}

    def idmm(col_ap, bank, deps):
        return S.op("tensor", lambda e, bank=bank: e.matmul(
            PBF[bank][:, 0:128], col_ap.broadcast_to([128, 128]), id32,
            start=True, stop=True),
            deps=list(deps) + [tc_id32, bkg[bank]])

    A32.push(); A16.push()
    xnT = A16.al(KD * 1027)
    wbA = [A16.al(KD * 128) for _ in range(2)]
    dtfm = A32.al(TOK)
    dtdecT = A32.al(TOK)
    decall = A32.al(CN * 128)
    scr1 = A32.al(TOK); scr2 = A32.al(TOK)
    cbuf = A32.al(1027)

    A16.push()
    xnbf = A16.al(DIM)
    xbuf = bigr[:, 0:DIM]
    lnm = A32.al(1); lnv = A32.al(1); lnq = A32.al(1)
    t_xn_parts = []
    prev_txn = None
    prev_tl_reader = None
    for c in range(CN + 1):
        halo = (c == CN)
        rows = 3 if halo else 128
        src = dxh if halo else dx[c * 128:(c + 1) * 128, :]
        tl = S.dma("sync", xbuf[0:rows, :], src, deps=[prev_tl_reader])
        tm0 = S.op("vector", lambda e, r=rows: e.tensor_reduce(
            lnm[0:r, :], xbuf[0:r, :], AX.X, ALU.add), deps=[tl])
        tq = S.op("scalar", lambda e, r=rows: e.activation(
            xnbf[0:r, :], xbuf[0:r, :], AF.Square, accum_out=lnv[0:r, :]),
            deps=[tl, prev_txn])
        S.op("vector", lambda e, r=rows: e.tensor_scalar(
            lnm[0:r, :], lnm[0:r, :], 1.0 / DIM, None, ALU.mult))
        S.op("vector", lambda e, r=rows: e.tensor_tensor(
            lnq[0:r, :], lnm[0:r, :], lnm[0:r, :], ALU.mult))
        tv = S.op("vector", lambda e, r=rows: e.tensor_scalar(
            lnv[0:r, :], lnv[0:r, :], 1.0 / DIM, None, ALU.mult), deps=[tq])
        S.op("vector", lambda e, r=rows: e.tensor_tensor(
            lnv[0:r, :], lnv[0:r, :], lnq[0:r, :], ALU.subtract))
        tvar = S.op("vector", lambda e, r=rows: e.engine_nop()) if False else None
        tr = S.op("scalar", lambda e, r=rows: e.activation(
            lnv[0:r, :], lnv[0:r, :], AF.Ln, bias=epsc[0:r, :]),
            deps=[tv, t_eps])
        tr2 = S.op("scalar", lambda e, r=rows: e.activation(
            lnv[0:r, :], lnv[0:r, :], AF.Exp, scale=-0.5), hard=[tr])
        txn = S.op("vector", lambda e, r=rows: e.tensor_scalar(
            xnbf[0:r, :], xbuf[0:r, :], lnm[0:r, :], lnv[0:r, :],
            ALU.subtract, ALU.mult), deps=[tr2, tq])
        prev_txn_parts = []
        if "xn0" in tap_outs and c == 0:
            S.dma("gpsimd", tap_outs["xn0"], xnbf, deps=[txn])
        for k in range(KD):
            col = k * 1027 + (0 if halo else 3 + c * 128)
            w = 3 if halo else 128
            bank = 6 + (k & 1)
            tp = S.op("tensor", lambda e, k=k, r=rows, bank=bank: e.transpose(
                PBB[bank][0:128, 0:r],
                xnbf[0:r, k * 128:(k + 1) * 128], idb[0:r, 0:r]),
                deps=[txn, tc_idb, bkg[bank]])
            te = S.op("vector", lambda e, col=col, w=w, bank=bank: e.tensor_copy(
                xnT[:, col:col + w], PBB[bank][0:128, 0:w]),
                deps=[tp])
            bkg[bank] = te
            prev_txn_parts.append(te)
        prev_txn = prev_txn_parts
        prev_tl_reader = txn
        t_xn_parts.extend(prev_txn_parts)
    t_xn = t_xn_parts[-1]
    A16.pop()

    if phases < 1:
        return _finish_early([t_xn])

    wb_guard = [None, None]
    main_guard = [None, None]
    tail_guard = [None, None]
    sstate = {"i": 0}

    def stream_ob(dramW, ob, conv, deps=(), stop_last=True):
        i = sstate["i"] & 1
        sstate["i"] += 1
        src = dramW[:, ob * 128:(ob + 1) * 128].rearrange(
            "(k d) o -> d k o", d=128)
        tw = S.dma("gpsimd", wbA[i][:, 0:KD * 128], src,
                   deps=[wb_guard[i]] + list(deps))
        pA_, pB_, pT_ = pb[2 * i], pb[2 * i + 1], pb[4 + i]
        t_mm = None
        for k in range(KD):
            lhs = wbA[i][:, k * 128:(k + 1) * 128]
            st = (k == 0)
            sp = (k == KD - 1) and stop_last
            base = k * 1027
            if conv:
                S.op("tensor", lambda e, lhs=lhs, st=st, sp=sp, base=base,
                     pA_=pA_: e.matmul(pA_[:], lhs, xnT[:, base:base + 512],
                                       start=st, stop=sp),
                     deps=[tw, t_xn, main_guard[i], tail_guard[i]])
                S.op("tensor", lambda e, lhs=lhs, st=st, sp=sp, base=base,
                     pB_=pB_: e.matmul(pB_[:], lhs,
                                       xnT[:, base + 512:base + 1024],
                                       start=st, stop=sp))
                t_mm = S.op("tensor", lambda e, lhs=lhs, st=st, sp=sp,
                            base=base, pT_=pT_: e.matmul(
                                pT_[:, 0:3], lhs,
                                xnT[:, base + 1024:base + 1027],
                                start=st, stop=sp))
            else:
                S.op("tensor", lambda e, lhs=lhs, st=st, sp=sp, base=base,
                     pA_=pA_: e.matmul(pA_[:], lhs, xnT[:, base + 3:base + 515],
                                       start=st, stop=sp),
                     deps=[tw, t_xn, main_guard[i]])
                t_mm = S.op("tensor", lambda e, lhs=lhs, st=st, sp=sp,
                            base=base, pB_=pB_: e.matmul(
                                pB_[:], lhs, xnT[:, base + 515:base + 1027],
                                start=st, stop=sp))
        wb_guard[i] = t_mm
        return i, t_mm

    def conv_silu(i, t_mm, co, dst16, deps=()):
        pA_, pB_, pT_ = pb[2 * i], pb[2 * i + 1], pb[4 + i]
        cw_sl = cwb[:, (co & 1) * 4:(co & 1) * 4 + 4]
        tcw = S.dma("sync", cw_sl, dcw[co], deps=[sstate.get(("cw", co & 1))])
        e0 = S.op("vector", lambda e: e.tensor_copy(cbuf[:, 0:512], pA_[:]),
                  deps=[t_mm, sstate.get("cbuf_g")], drain=False)
        e1 = S.op("vector", lambda e: e.tensor_copy(cbuf[:, 512:1024], pB_[:]),
                  drain=False)
        e2 = S.op("vector", lambda e: e.tensor_copy(cbuf[:, 1024:1027],
                                                    pT_[:, 0:3]), drain=False)
        tc0 = S.op("vector", lambda e: e.tensor_scalar(
            scr1[:, 0:1024], cbuf[:, 0:1024], cw_sl[:, 0:1], cbc[:, co:co + 1],
            ALU.mult, ALU.add),
            deps=[e0, e1, e2, tcw, tc_cbc, sstate.get("scr1_g")])
        for k in range(1, 4):
            tc0 = S.op("vector", lambda e, k=k: e.scalar_tensor_tensor(
                scr1[:, 0:1024], cbuf[:, k:k + 1024], cw_sl[:, k:k + 1],
                scr1[:, 0:1024], ALU.mult, ALU.add))
        sstate["cbuf_g"] = tc0
        sstate[("cw", co & 1)] = tc0
        tsl = S.op("scalar", lambda e: e.activation(
            dst16, scr1[:, 0:1024], AF.Silu), deps=[tc0] + list(deps))
        sstate["scr1_g"] = tsl
        return tsl, [e0, e1, e2]

    i, t_mm = stream_ob(dwpT, OB_DT, False)
    pA_, pB_ = pb[2 * i], pb[2 * i + 1]
    if phases == 1 and sub <= -1:
        tev0 = S.op("vector", lambda e: e.tensor_copy(dtfm[:, 0:512], pA_[:]),
                    deps=[t_mm])
        tev1 = S.op("vector", lambda e: e.tensor_copy(dtfm[:, 512:1024], pB_[:]))
        return _finish_early([tev1])
    dt_ev = []
    for half, p_ in ((0, pA_), (1, pB_)):
        sl = slice(half * 512, (half + 1) * 512)
        ta_ = S.op("scalar", lambda e, p_=p_, sl=sl: e.activation(
            scr1[:, sl], p_[:], AF.Abs, bias=dbc), deps=[t_mm, tc_dbc])
        _h = ta_
        if sub == 10:
            dt_ev.append(S.op("vector", lambda e, sl=sl: e.tensor_copy(
                dtfm[:, sl], scr1[:, sl]), deps=[ta_]))
            continue
        tb_ = S.op("scalar", lambda e, sl=sl: e.activation(
            scr1[:, sl], scr1[:, sl], AF.Exp, scale=-1.0), hard=[_h])
        tc2 = S.op("scalar", lambda e, sl=sl: e.activation(
            scr1[:, sl], scr1[:, sl], AF.Ln, bias=ones[:, 0:1]),
            deps=[t_ones], hard=[tb_])
        if sub == 11:
            dt_ev.append(S.op("vector", lambda e, sl=sl: e.tensor_copy(
                dtfm[:, sl], scr1[:, sl]), deps=[tc2]))
            continue
        S.op("vector", lambda e, p_=p_, sl=sl: e.tensor_scalar(
            scr2[:, sl], p_[:], dbc, 0.0, ALU.add, ALU.max),
            deps=[t_mm, tc_dbc, ta_])
        dt_ev.append(S.op("vector", lambda e, sl=sl: e.tensor_tensor(
            dtfm[:, sl], scr1[:, sl], scr2[:, sl], ALU.add), deps=[tc2]))
    main_guard[i] = dt_ev
    t_dt = dt_ev[-1]
    sstate["scr1_g"] = t_dt
    if "dtfm" in tap_outs:
        S.dma("sync", tap_outs["dtfm"], dtfm, deps=[t_dt])
    if phases == 1 and sub <= 0 or sub in (10, 11):
        return _finish_early([t_dt])

    t1 = S.op("scalar", lambda e: e.activation(acol, alc, AF.Exp),
              deps=[tc_alc])
    t_a = S.op("vector", lambda e: e.tensor_scalar(
        acol, acol, -1.0, None, ALU.mult), deps=[t1])
    t_adt = S.op("vector", lambda e: e.tensor_scalar(
        scr1[:, 0:TOK], dtfm, acol, None, ALU.mult), deps=[t_a, t_dt])
    csfm = scr2
    tcs = []
    for c in range(CN):
        sl = slice(c * 128, (c + 1) * 128)
        tcs.append(S.op("vector", lambda e, sl=sl: e.tensor_tensor_scan(
            csfm[:, sl], ones[:, 0:128], scr1[:, sl], 0.0, ALU.mult, ALU.add),
            deps=[t_adt, t_ones]))
    sstate["scr1_g"] = tcs[-1]
    if phases == 1 and sub <= 1:
        return _finish_early([tcs[-1]])
    t_csT = []
    for c in range(CN):
        sl = slice(c * 128, (c + 1) * 128)
        tp = S.op("tensor", lambda e, sl=sl: e.transpose(
            pX[:, 0:128], csfm[:, sl], id32),
            deps=[tcs[c], tc_id32, bkg[6]])
        te = S.op("vector", lambda e, sl=sl: e.tensor_copy(
            csT[:, sl], pX[:, 0:128]), deps=[tp])
        bkg[6] = te
        t_csT.append(te)
        tp2 = S.op("tensor", lambda e, sl=sl: e.transpose(
            pS_[:, 0:128], dtfm[:, sl], id32),
            deps=[t_dt, bkg[7]])
        te2 = S.op("vector", lambda e, sl=sl: e.tensor_copy(
            dtT[:, sl], pS_[:, 0:128]), deps=[tp2])
        bkg[7] = te2
    t_hilo = []
    for c in range(CN):
        sl = slice(c * 128, (c + 1) * 128)
        S.op("vector", lambda e, sl=sl: e.tensor_copy(cshiT[:, sl], csT[:, sl]),
             deps=[t_csT[c]])
        S.op("vector", lambda e, sl=sl: e.tensor_tensor(
            scr1[:, sl], csT[:, sl], cshiT[:, sl], ALU.subtract))
        t_hilo.append(S.op("vector", lambda e, sl=sl: e.tensor_copy(
            csloT[:, sl], scr1[:, sl])))
    sstate["scr1_g"] = t_hilo[-1]
    t_expcs = S.op("scalar", lambda e: e.activation(expcsT, csT, AF.Exp),
                   deps=[t_csT[-1]])
    if phases == 1 and sub <= 2:
        return _finish_early([t_expcs, t_hilo])
    t_tc = S.op("vector", lambda e: e.tensor_copy(tcum, csfm[:, 127::128]),
                deps=tcs)
    t_tcum = S.op("vector", lambda e: e.tensor_tensor_scan(
        tcum2, ones[:, 0:CN], tcum, 0.0, ALU.mult, ALU.add), deps=[t_tc])
    t_dcol = S.op("scalar", lambda e: e.activation(
        dcolc, tcum2[:, CN - 1:CN], AF.Exp), deps=[t_tcum])
    t_lam1 = S.op("scalar", lambda e: e.activation(
        lamc[:, 1:CN], tcum2[:, 0:CN - 1], AF.Exp), deps=[t_tcum])
    t_lam0 = S.op("vector", lambda e: e.memset(lamc[:, 0:1], 1.0))
    t_lam = [t_lam0, t_lam1]
    t_etc = S.op("scalar", lambda e: e.activation(
        etc_, csfm[:, 127::128], AF.Exp), deps=tcs)
    if phases == 1 and sub <= 3:
        return _finish_early([t_dcol, t_lam, t_etc])
    t_decall = []
    for c in range(CN):
        tp = idmm(etc_[:, c:c + 1], 6, [t_etc])
        te = S.op("vector", lambda e, c=c: e.tensor_copy(
            decall[:, c * 128:(c + 1) * 128], pX[:, 0:128]),
            deps=[tp])
        bkg[6] = te
        t_decall.append(te)
        sl = slice(c * 128, (c + 1) * 128)
        tp2 = idmm(csfm[:, c * 128 + 127:c * 128 + 128], 7, [tcs[c]])
        tb = S.op("vector", lambda e, sl=sl: e.tensor_tensor(
            scr1[:, sl], pS_[:, 0:128], csT[:, sl],
            ALU.subtract), deps=[tp2, t_csT[c], t_hilo])
        bkg[7] = tb
        tc3 = S.op("scalar", lambda e, sl=sl: e.activation(
            scr1[:, sl], scr1[:, sl], AF.Exp), deps=[tb])
        t_dtdec = S.op("vector", lambda e, sl=sl: e.tensor_tensor(
            dtdecT[:, sl], dtT[:, sl], scr1[:, sl], ALU.mult), deps=[tc3])
    sstate["scr1_g"] = t_dtdec
    tpD = idmm(dc_, 6, [tc_dc])
    t_dful = S.op("vector", lambda e: e.tensor_copy(dful, pX[:, 0:128]),
                  deps=[tpD])
    bkg[6] = t_dful
    for nm, buf, dep in (("csT", csT, t_csT), ("expcsT", expcsT, [t_expcs]),
                         ("dtT", dtT, t_csT), ("dtdecT", dtdecT, [t_dtdec]),
                         ("decall", decall, t_decall), ("lam", lamc, t_lam),
                         ("dcol", dcolc, [t_dcol])):
        if nm in tap_outs:
            S.dma("sync", tap_outs[nm], buf, deps=list(dep) + [t_dtdec])
    if phases < 2:
        return _finish_early([t_dtdec, t_dful, t_expcs])

    A16.push()
    cfm = A16.al(G * TOK)
    bfmg = A16.al(TOK)
    btmg = A16.al(CN * 128)
    gmtb = [A16.al(128) for _ in range(2)]
    t_cfm = [None] * G
    t_cfm_sp = [None] * G
    for g in range(G):
        i, t_mm = stream_ob(dwpT, OB_C0 + g, True)
        dst = cfm[:, g * TOK:(g + 1) * TOK]
        tsl, ev = conv_silu(i, t_mm, 72 + g, dst,
                            deps=[sstate.get(("cfm_g", g))])
        main_guard[i] = ev[1:3]
        tail_guard[i] = ev[0]
        t_cfm[g] = tsl
        t_cfm_sp[g] = S.dma("sync", cfmD[:, g].transpose([1, 0, 2]), dst,
                            deps=[tsl])
    if "cfm" in tap_outs:
        S.dma("sync", tap_outs["cfm"], cfm, deps=t_cfm)
    gm_guard = {}
    t_btm_sp = []
    t_gmt_sp = []
    for g in range(G):
        i, t_mm = stream_ob(dwpT, OB_B0 + g, True)
        tsl, ev = conv_silu(i, t_mm, 64 + g, bfmg,
                            deps=[sstate.get("bfm_g")])
        main_guard[i] = ev[1:3]
        tail_guard[i] = ev[0]
        bconsumers = []
        te2 = None
        for c in range(CN):
            tg = S.op("tensor", lambda e, g=g, c=c: e.matmul(
                pX[:, 0:128],
                bfmg[:, c * 128:(c + 1) * 128],
                cfm[:, g * TOK + c * 128:g * TOK + (c + 1) * 128],
                start=True, stop=True),
                deps=[tsl, t_cfm[g], bkg[6]])
            par = c & 1
            te = S.op("vector", lambda e, par=par: e.tensor_tensor(
                gmtb[par], pX[:, 0:128], mut, ALU.mult),
                deps=[tg, tc_mut, gm_guard.get(("gm", par))])
            bkg[6] = te
            tgs = S.dma("sync", gmtD[c, g], gmtb[par], deps=[te])
            gm_guard[("gm", par)] = tgs
            t_gmt_sp.append(tgs)
            bconsumers.append(tg)
            if "gmt0" in tap_outs and g == 0 and c == 0:
                S.dma("sync", tap_outs["gmt0"], gmtb[par], deps=[te])
            tb2 = S.op("tensor", lambda e, c=c: e.transpose(
                pSb[:, 0:128], bfmg[:, c * 128:(c + 1) * 128], idb),
                deps=[tsl, bkg[7]])
            te2 = S.op("vector", lambda e, c=c: e.tensor_copy(
                btmg[:, c * 128:(c + 1) * 128], pSb[:, 0:128]), deps=[tb2])
            bkg[7] = te2
            bconsumers.append(tb2)
        tbs = S.dma("sync", btmD[g], btmg, deps=[te2, gm_guard.get("btm_sp")])
        gm_guard["btm_sp"] = tbs
        t_btm_sp.append(tbs)
        if "btm0" in tap_outs and g == 0:
            S.dma("sync", tap_outs["btm0"], btmg[:, 0:128], deps=[te2])
        sstate["bfm_g"] = bconsumers
    A16.pop()
    if phases < 3:
        return _finish_early([t_btm_sp, t_gmt_sp, t_cfm_sp])

    fenceA2 = S.fence()
    A32.push(); A16.push()
    logit = A32.al(TOK)
    zT0 = A32.al(CN * 8)
    zA = A32.al(8); zB = A32.al(8); zC = A32.al(8)
    tauc = A32.al(1)
    hs32 = A32.al(TOK)
    sgbuf = A32.al(TOK)
    wr_sb = A16.al(KD * 8)
    wTb = A16.al(CN * 8)
    hwb = A16.al(TOK)
    gbuf16 = A16.al(TOK)
    sgn16 = A16.al(TOK)

    t_wr = S.dma("gpsimd", wr_sb, dwrT.rearrange("(k d) o -> d k o", d=128),
                 deps=fenceA2)
    t_log_ev = []
    for half in range(2):
        tm2 = None
        for k in range(KD):
            st, sp = (k == 0), (k == KD - 1)
            base = k * 1027 + 3 + half * 512
            tm2 = S.op("tensor", lambda e, k=k, st=st, sp=sp, base=base:
                       e.matmul(pS_[0:8, 0:512], wr_sb[:, k * 8:(k + 1) * 8],
                                xnT[:, base:base + 512], start=st, stop=sp),
                       deps=[t_wr, t_xn, t_log_ev, bkg[7]])
        te = S.op("scalar", lambda e, half=half: e.activation(
            logit[0:8, half * 512:(half + 1) * 512], pS_[0:8, 0:512],
            AF.Identity, bias=brc), deps=[tm2, tc_brc])
        bkg[7] = te
        t_log_ev.append(te)
    if "logit" in tap_outs:
        S.dma("sync", tap_outs["logit"], logit[0:8, :], deps=t_log_ev)
    SORT_STAGES = ["c", "b", "s3", "a", "s5", "s6"]
    t_w = [None] * CN
    pz_guard = [None]
    for c in range(CN):
        tpz = S.op("tensor", lambda e, c=c: e.transpose(
            pS_[:, 0:8], logit[0:8, c * 128:(c + 1) * 128], id32[0:8, 0:8]),
            deps=[t_log_ev, bkg[7]])
        tz0 = S.op("vector", lambda e, c=c: e.tensor_copy(
            zT0[:, c * 8:(c + 1) * 8], pS_[:, 0:8]), deps=[tpz])
        bkg[7] = tz0
        pz_guard[0] = tz0
        S.op("vector", lambda e, c=c: e.tensor_copy(
            zA[:, 0:8], zT0[:, c * 8:(c + 1) * 8]))
        bufs = [zA, zB, zC]
        cur = 0

        def v4(t):
            return t[:, 0:8].rearrange("p (a b c2) -> p a b c2", b=2, c2=2)
        for kind in SORT_STAGES:
            ca, na = v4(bufs[cur]), v4(bufs[(cur + 1) % 3])
            cu, nx = bufs[cur], bufs[(cur + 1) % 3]
            if kind == "c":
                lo, hi = ca[:, :, :, 0:1], ca[:, :, :, 1:2]
                lod, hid = na[:, :, :, 0:1], na[:, :, :, 1:2]
                uns = []
            elif kind == "b":
                lo, hi = ca[:, :, 0:1, :], ca[:, :, 1:2, :]
                lod, hid = na[:, :, 0:1, :], na[:, :, 1:2, :]
                uns = []
            elif kind == "s3":
                lo, hi = ca[:, :, 0:1, 1:2], ca[:, :, 1:2, 0:1]
                lod, hid = na[:, :, 0:1, 1:2], na[:, :, 1:2, 0:1]
                uns = [(ca[:, :, 0:1, 0:1], na[:, :, 0:1, 0:1]),
                       (ca[:, :, 1:2, 1:2], na[:, :, 1:2, 1:2])]
            elif kind == "a":
                lo, hi = ca[:, 0:1, :, :], ca[:, 1:2, :, :]
                lod, hid = na[:, 0:1, :, :], na[:, 1:2, :, :]
                uns = []
            elif kind == "s5":
                lo, hi = ca[:, 0:1, 1:2, :], ca[:, 1:2, 0:1, :]
                lod, hid = na[:, 0:1, 1:2, :], na[:, 1:2, 0:1, :]
                uns = [(ca[:, 0:1, 0:1, :], na[:, 0:1, 0:1, :]),
                       (ca[:, 1:2, 1:2, :], na[:, 1:2, 1:2, :])]
            else:
                lo, hi = cu[:, 1:7:2], cu[:, 2:8:2]
                lod, hid = nx[:, 1:7:2], nx[:, 2:8:2]
                uns = [(cu[:, 0:1], nx[:, 0:1]), (cu[:, 7:8], nx[:, 7:8])]
            S.op("vector", lambda e, lo=lo, hi=hi, lod=lod: e.tensor_tensor(
                lod, lo, hi, ALU.max))
            S.op("vector", lambda e, lo=lo, hi=hi, hid=hid: e.tensor_tensor(
                hid, lo, hi, ALU.min))
            for (ua, ub) in uns:
                S.op("vector", lambda e, ua=ua, ub=ub: e.tensor_copy(ub, ua))
            cur = (cur + 1) % 3
        cu, nx = bufs[cur], bufs[(cur + 1) % 3]
        S.op("vector", lambda e, cu=cu, nx=nx: e.tensor_tensor_scan(
            nx[:, 0:8], ones[:, 0:8], cu[:, 0:8], 0.0, ALU.mult, ALU.add))
        S.op("vector", lambda e, nx=nx: e.tensor_scalar(
            nx[:, 0:8], nx[:, 0:8], -1.0, None, ALU.add))
        S.op("vector", lambda e, nx=nx: e.tensor_tensor(
            nx[:, 0:8], nx[:, 0:8], invk, ALU.mult), deps=[tc_invk])
        S.op("vector", lambda e, nx=nx: e.tensor_reduce(
            tauc, nx[:, 0:8], AX.X, ALU.max))
        t_w[c] = S.op("vector", lambda e, c=c: e.tensor_scalar(
            wTb[:, c * 8:(c + 1) * 8], zT0[:, c * 8:(c + 1) * 8], tauc, 0.0,
            ALU.subtract, ALU.max))
    if "wT0" in tap_outs:
        S.dma("gpsimd", tap_outs["wT0"], wTb[:, 0:8], deps=[t_w[0]])
    A16.push()
    la_sb = A16.al(KD * 128)
    t_la = S.dma("gpsimd", la_sb, dlaT.rearrange("(k d) o -> d k o", d=128),
                 deps=fenceA2)
    t_h_ev = []
    for half in range(2):
        tm2 = None
        for k in range(KD):
            st, sp = (k == 0), (k == KD - 1)
            base = k * 1027 + 3 + half * 512
            tm2 = S.op("tensor", lambda e, k=k, st=st, sp=sp, base=base:
                       e.matmul(pX[:], la_sb[:, k * 128:(k + 1) * 128],
                                xnT[:, base:base + 512], start=st, stop=sp),
                       deps=[t_la, t_xn, t_h_ev, bkg[6]])
        teh = S.op("vector", lambda e, half=half: e.tensor_copy(
            hs32[:, half * 512:(half + 1) * 512], pX[:]), deps=[tm2],
            drain=False)
        bkg[6] = teh
        t_h_ev.append(teh)
    if "hff" in tap_outs:
        S.dma("sync", tap_outs["hff"], hs32, deps=t_h_ev)
    if "lasb" in tap_outs:
        S.dma("sync", tap_outs["lasb"], la_sb, deps=[t_la] + t_h_ev)
    t_hw = []
    wrep_l = A16.al(128)
    for c in range(CN):
        twc = S.op("vector", lambda e, c=c: e.tensor_copy(
            wrep_l,
            wTb[:, c * 8:(c + 1) * 8].unsqueeze(2).broadcast_to([128, 8, 16])),
            deps=[t_w[c], sstate.get("wrep_g")])
        twr2 = S.op("tensor", lambda e, c=c: e.transpose(
            pSb[:, 0:128], wrep_l, idb),
            deps=[twc, tc_idb, bkg[7]])
        sstate["wrep_g"] = twr2
        th2 = S.op("vector", lambda e, c=c: e.tensor_tensor(
            hwb[:, c * 128:(c + 1) * 128], hs32[:, c * 128:(c + 1) * 128],
            pSb[:, 0:128], ALU.mult), deps=[twr2, t_h_ev], drain=False)
        bkg[7] = th2
        t_hw.append(th2)
    A16.pop()
    A16.push()
    lb_sb = A16.al(32 * 128)
    t_lb = S.dma("gpsimd", lb_sb, dlbF, deps=S.fence())
    g_guard = [None]
    for ob in range(32):
        i, t_mm = stream_ob(dwbT, ob, False, stop_last=False)
        pA_, pB_ = pb[2 * i], pb[2 * i + 1]
        ev = []
        for half, p_ in ((0, pA_), (1, pB_)):
            tl2 = S.op("tensor", lambda e, ob=ob, half=half, p_=p_: e.matmul(
                p_[:], lb_sb[:, ob * 128:(ob + 1) * 128],
                hwb[:, half * 512:(half + 1) * 512], start=False, stop=True),
                deps=[t_lb] + t_hw)
            ev.append(S.op("scalar", lambda e, ob=ob, half=half, p_=p_:
                           e.activation(gbuf16[:, half * 512:(half + 1) * 512],
                                        p_[:], AF.Silu, bias=bbc[:, ob:ob + 1]),
                           deps=[tl2, tc_bbc, g_guard[0]]))
        wb_guard[i] = tl2
        main_guard[i] = ev
        tg2 = S.dma("sync", gD[ob], gbuf16, deps=ev)
        g_guard[0] = tg2
    t_g_done = g_guard[0]
    sg_guard = [None]
    for hb in range(HBLK):
        i, t_mm = stream_ob(dwpT, OB_GATE0 + hb, False)
        pA_, pB_ = pb[2 * i], pb[2 * i + 1]
        ev = []
        for half, p_ in ((0, pA_), (1, pB_)):
            ev.append(S.op("scalar", lambda e, p_=p_, half=half: e.activation(
                sgbuf[:, half * 512:(half + 1) * 512], p_[:], AF.Silu),
                deps=[t_mm, sg_guard[0]]))
        tn = S.op("vector", lambda e, hb=hb: e.tensor_scalar(
            sgn16, sgbuf, nwc[:, hb:hb + 1], None, ALU.mult),
            deps=ev + [tc_nwc, sg_guard[0]], drain=False)
        main_guard[i] = tn
        tsp = S.dma("sync", sgD[hb].transpose([1, 0, 2]), sgn16, deps=[tn])
        sg_guard[0] = tsp
    t_sg_done = sg_guard[0]
    A16.pop()
    A32.pop(); A16.pop()
    if phases < 4:
        return _finish_early([t_g_done, t_sg_done])

    fenceA3 = S.fence()
    A16.push()
    xf16 = A16.al(TOK)
    btmg2 = A16.al(CN * 128)
    xs2b = A16.al(128)
    szst = [A16.al(128) for _ in range(2)]
    t_zero_srun = S.op("vector", lambda e: e.memset(srun, 0.0),
                       deps=[t_xn, t_dt])
    t_sz = {}
    a3 = {("pst", 0): t_hw, ("pst", 1): t_hw}
    t_btm_ld = None
    xf_guard = [None]
    for hb in range(HBLK):
        g = hb // 8
        if hb % 8 == 0:
            t_btm_ld = S.dma("sync", btmg2, btmD[g],
                             deps=[t_btm_sp[g], a3.get("btm_rd")] + fenceA3)
        i, t_mm = stream_ob(dwpT, OB_X0 + hb, True)
        tsl, ev = conv_silu(i, t_mm, hb, xf16, deps=[xf_guard[0]])
        main_guard[i] = ev[1:3]
        tail_guard[i] = ev[0]
        txf = S.dma("sync", xfD[hb].transpose([1, 0, 2]), xf16, deps=[tsl])
        xf_guard[0] = txf
        for c in range(CN):
            par = c & 1
            tcast = S.op("vector", lambda e, hb=hb, par=par: e.tensor_copy(
                szst[par], srun[:, hb * 128:(hb + 1) * 128]),
                deps=[t_zero_srun, a3.get(("srun", hb)),
                      t_sz.get((hb, c - 2))], drain=False)
            tspl = S.dma("sync", szD[c, hb], szst[par], deps=[tcast])
            t_sz[(hb, c)] = tspl
            dv = decall[:, c * 128 + 2 * hb:c * 128 + 2 * hb + 2]
            S.op("vector", lambda e, hb=hb, dv=dv: e.tensor_tensor(
                srun[:, hb * 128:(hb + 1) * 128],
                srun[:, hb * 128:(hb + 1) * 128],
                dv.unsqueeze(2).broadcast_to([128, 2, 64]), ALU.mult),
                deps=[t_decall], drain=False)
            bA = 6 + (c & 1)
            bB = 6 + ((c + 1) & 1)
            tpx = S.op("tensor", lambda e, c=c, bA=bA: e.transpose(
                PBB[bA][:, 0:128], xf16[:, c * 128:(c + 1) * 128], idb),
                deps=[tsl, bkg[bA]])
            tadds = []
            txss = []
            for hh in range(2):
                h = 2 * hb + hh
                txs = S.op("vector", lambda e, c=c, h=h, hh=hh, bA=bA:
                           e.tensor_scalar(
                               xs2b[:, hh * 64:hh * 64 + 64],
                               PBB[bA][:, hh * 64:hh * 64 + 64],
                               dtdecT[:, c * 128 + h:c * 128 + h + 1], None,
                               ALU.mult),
                           deps=[tpx, t_dtdec, a3.get(("xs2", hh))],
                           drain=False)
                txss.append(txs)
                bank = bB if hh == 0 else bA
                tst = S.op("tensor", lambda e, c=c, hh=hh, bank=bank: e.matmul(
                    PBF[bank][:, 0:64], btmg2[:, c * 128:(c + 1) * 128],
                    xs2b[:, hh * 64:(hh + 1) * 64], start=True, stop=True),
                    deps=[txs, t_btm_ld, bkg[bank]] +
                         ([txss[0], txss[1]] if hh == 1 else []))
                a3[("xs2", hh)] = tst
                ta2 = S.op("vector", lambda e, hb=hb, hh=hh, bank=bank:
                           e.tensor_tensor(
                               srun[:, hb * 128 + hh * 64:hb * 128 + hh * 64 + 64],
                               srun[:, hb * 128 + hh * 64:hb * 128 + hh * 64 + 64],
                               PBF[bank][:, 0:64], ALU.add), deps=[tst],
                           drain=False)
                bkg[bank] = ta2
                a3[("pst", hh)] = ta2
                tadds.append(ta2)
            bkg[bA] = [txss[1], tadds[1]]
            a3[("srun", hb)] = tadds
        a3["btm_rd"] = a3.get("px_rd")
    t_srun = [a3[("srun", hb)] for hb in range(HBLK)]
    A16.pop()
    A32.pop(); A16.pop()
    if "srun" in tap_outs:
        S.dma("sync", tap_outs["srun"], srun, deps=t_srun)
    if phases < 5:
        return _finish_early(t_srun + [xf_guard[0]])

    fenceCC = S.fence()
    A16.push()
    s0st = A16.al(N * P)
    s0b = [A16.al(N * P) for _ in range(2)]
    d1c = A32.al(1); d2c = A32.al(1)
    dful1 = A32.al(128); dful2 = A32.al(128)
    tS0c = S.op("vector", lambda e: e.tensor_copy(s0st, srun),
                deps=t_srun + fenceCC)
    tSo = S.dma("sync", ccSi, s0st, deps=[tS0c])
    tdo = S.dma("sync", ccdi, dcolc, deps=[t_dcol])
    rg = [[0, 1, 2, 3], [4, 5, 6, 7]]
    tcc1 = S.cc(lambda e: e.collective_compute(
        "AllGather", ALU.bypass, replica_groups=rg,
        ins=[ccSi.opt()], outs=[ccSo.opt()]), deps=[tSo])
    tcc2 = S.cc(lambda e: e.collective_compute(
        "AllGather", ALU.bypass, replica_groups=rg,
        ins=[ccdi.opt()], outs=[ccdo.opt()]), deps=[tdo])
    tl0 = S.dma("sync", s0b[0], ccSo[0:128, :], deps=[tcc1])
    tl1 = S.dma("sync", s0b[1], ccSo[128:256, :], deps=[tcc1])
    tld1 = S.dma("sync", d1c, ccdo[128:256, :], deps=[tcc2])
    tld2 = S.dma("sync", d2c, ccdo[256:384, :], deps=[tcc2])
    tfull = []
    for (dcx, dst, mA, mB, tld) in ((d1c, dful1, 0, 1, tld1),
                                    (d2c, dful2, 3, 4, tld2)):
        tpD2 = idmm(dcx, 6, [tld])
        tev = S.op("vector", lambda e, dst=dst, mA=mA, mB=mB: e.tensor_scalar(
            dst, pX[:, 0:128], ccm[:, mA:mA + 1], ccm[:, mB:mB + 1],
            ALU.mult, ALU.add), deps=[tpD2, tc_ccm])
        bkg[6] = tev
        tfull.append(tev)
    S.op("vector", lambda e: e.tensor_copy(srun, s0b[0]),
         deps=[tl0, tS0c] + t_srun)
    S.op("vector", lambda e: e.tensor_tensor(
        srun, srun, dful1.unsqueeze(2).broadcast_to([128, 128, 64]), ALU.mult),
        deps=[tfull[0]])
    tch = S.op("vector", lambda e: e.scalar_tensor_tensor(
        srun, s0b[1], ccm[:, 2:3], srun, ALU.mult, ALU.add), deps=[tl1])
    tl2b = S.dma("sync", s0b[0], ccSo[256:384, :], deps=[tcc1, tch])
    S.op("vector", lambda e: e.tensor_tensor(
        srun, srun, dful2.unsqueeze(2).broadcast_to([128, 128, 64]), ALU.mult),
        deps=[tfull[1]])
    S.op("vector", lambda e: e.scalar_tensor_tensor(
        srun, s0b[0], ccm[:, 5:6], srun, ALU.mult, ALU.add), deps=[tl2b])
    t_sic = S.op("vector", lambda e: e.tensor_scalar(
        srun, srun, ccm[:, 6:7], None, ALU.mult))
    A16.pop()
    if "sincore" in tap_outs:
        S.dma("sync", tap_outs["sincore"], srun, deps=[t_sic])
    if phases < 6:
        return _finish_early([t_sic])

    fenceB = S.fence()
    A32.push(); A16.push()
    lamf = A32.al(128)
    scrB = [A32.al(128) for _ in range(4)]
    scrL = [A32.al(128) for _ in range(4)]
    scrT = [A32.al(1024) for _ in range(2)]
    scrSq = A32.al(128); sqac = A32.al(1)
    sinc = A16.al(N * P)
    szb = A16.al(N * P)
    zc16 = A16.al(HBLK * 128)
    gmtc = A16.al(G * 128)
    cfmc = A16.al(G * 128)
    xtm = [A16.al(1024) for _ in range(2)]
    sgs = [A16.al(1024) for _ in range(2)]
    xdt16 = [A16.al(1024) for _ in range(2)]
    mtb16 = [A16.al(16 * 128) for _ in range(2)]
    ya32 = [A32.al(1024) for _ in range(2)]
    t_ssq0 = S.op("vector", lambda e: e.memset(ssq_sb, 0.0))
    pRb = [pb[0], pb[3], pb[7]]
    pYdb = [pb[1], pb[4]]
    pYob = [pb[2], pb[5]]
    gb = {}
    for _s in range(4):
        for _k in ("pR", "scrB", "scrL"):
            gb[(_k, _s)] = fenceB
    for _p in range(2):
        for _k in ("pYd", "pYo", "xtm", "sgs", "xdt16", "mtb16", "ya32",
                   "scrT"):
            gb[(_k, _p)] = fenceB
    t_zsp = [None] * CN

    def b_group(c, g, tb2, tcf, tgm):
        gp = g & 1
        tx = S.dmaT("scalar", xtm[gp].rearrange("p (b q) -> p b q", b=8),
                    xfD[g * 8:(g + 1) * 8, c], deps=[xf_guard[0],
                                                     gb[("xtm", gp)]]) \
            if False else None
        txs = []
        tgs = []
        for j in range(8):
            txs.append(S.dmaT("scalar", xtm[gp][:, j * 128:(j + 1) * 128],
                              xfD[g * 8 + j, c],
                              deps=[xf_guard[0], gb[("xtm", gp)]]))
            tgs.append(S.dmaT("scalar", sgs[gp][:, j * 128:(j + 1) * 128],
                              sgD[g * 8 + j, c],
                              deps=[t_sg_done, gb[("sgs", gp)]]))
        txd = []
        for j in range(8):
            col = c * 128 + g * 16 + 2 * j
            txd.append(S.op("vector", lambda e, gp=gp, j=j, col=col:
                            e.tensor_tensor(
                                xdt16[gp][:, j * 128:(j + 1) * 128],
                                xtm[gp][:, j * 128:(j + 1) * 128],
                                dtT[:, col:col + 2].unsqueeze(2)
                                .broadcast_to([128, 2, 64]), ALU.mult),
                            deps=[txs[j], gb[("xdt16", gp)]], drain=False))
        tmts = []
        for i in range(16):
            h = g * 16 + i
            col = c * 128 + h
            s3 = i % 3
            s4 = i & 3
            S.op("tensor", lambda e, col=col, s3=s3: e.matmul(
                pRb[s3][:, 0:128],
                cshiT[:, col:col + 1].broadcast_to([128, 128]), idb,
                start=True, stop=False),
                deps=[t_hilo, gb[("pR", s3)], tc_idb])
            tR = S.op("tensor", lambda e, col=col, s3=s3: e.matmul(
                pRb[s3][:, 0:128],
                csloT[:, col:col + 1].broadcast_to([128, 128]), idb,
                start=False, stop=True))
            tcl = S.op("vector", lambda e, col=col, s3=s3, s4=s4:
                       e.tensor_scalar(
                           scrB[s4], pRb[s3][:, 0:128], csT[:, col:col + 1],
                           0.0, ALU.subtract, ALU.min),
                       deps=[tR, t_csT, gb[("scrB", s4)]], drain=False)
            gb[("pR", s3)] = tcl
            tlt = S.op("scalar", lambda e, s4=s4: e.activation(
                scrL[s4], scrB[s4], AF.Exp),
                deps=[tcl, gb[("scrL", s4)]], drain=False)
            gb[("scrB", s4)] = tlt
            tmt = S.op("gpsimd", lambda e, g=g, gp=gp, i=i, s4=s4:
                       e.tensor_tensor(
                           mtb16[gp][:, i * 128:(i + 1) * 128], scrL[s4],
                           gmtc[:, g * 128:(g + 1) * 128], ALU.mult),
                       deps=[tlt, tgm, gb[("mtb16", gp)]], drain=False)
            gb[("scrL", s4)] = tmt
            tmts.append(tmt)
        tyos = []
        for half8 in range(2):
            ybk = half8
            tyo = S.op("tensor", lambda e, c=c, g=g, half8=half8, ybk=ybk:
                       e.matmul(
                           pYob[ybk][:],
                           cfmc[:, g * 128:(g + 1) * 128],
                           sinc[:, (g * 16 + half8 * 8) * 64:
                                (g * 16 + half8 * 8) * 64 + 512],
                           start=True, stop=True),
                       deps=[tb2, tcf, gb[("pYo", ybk)]])
            tyos.append(tyo)
        tt1s = []
        for i in range(16):
            col = c * 128 + g * 16 + i
            ybk = i // 8
            tt1s.append(S.op("scalar", lambda e, i=i, col=col, ybk=ybk, gp=gp:
                             e.activation(
                                 scrT[gp][:, i * 64:(i + 1) * 64],
                                 pYob[ybk][:, (i % 8) * 64:(i % 8) * 64 + 64],
                                 AF.Copy, scale=expcsT[:, col:col + 1]),
                             deps=[tyos[i // 8], t_expcs,
                                   gb[("scrT", gp)]], drain=False))
        gb[("pYo", 0)] = tt1s[7]
        gb[("pYo", 1)] = tt1s[15]
        ty1s = []
        for i in range(16):
            yp = i & 1
            tyd = S.op("tensor", lambda e, gp=gp, i=i, yp=yp: e.matmul(
                pYdb[yp][:, 0:64], mtb16[gp][:, i * 128:(i + 1) * 128],
                xdt16[gp][:, i * 64:(i + 1) * 64], start=True, stop=True),
                deps=[tmts[i], txd[i // 2], gb[("pYd", yp)]])
            h = g * 16 + i
            ty1 = S.op("vector", lambda e, gp=gp, i=i, h=h, yp=yp:
                       e.scalar_tensor_tensor(
                           ya32[gp][:, i * 64:(i + 1) * 64],
                           xtm[gp][:, i * 64:(i + 1) * 64],
                           dful[:, h:h + 1], pYdb[yp][:, 0:64],
                           ALU.mult, ALU.add),
                       deps=[tyd, t_dful, gb[("ya32", gp)]], drain=False)
            gb[("pYd", yp)] = ty1
            ty1s.append(ty1)
        gb[("mtb16", gp)] = ty1s[-1]
        gb[("xdt16", gp)] = ty1s[-1]
        tz_list = []
        tyb_list = []
        for j in range(8):
            tyb = S.op("vector", lambda e, gp=gp, j=j: e.tensor_tensor(
                ya32[gp][:, j * 128:(j + 1) * 128],
                ya32[gp][:, j * 128:(j + 1) * 128],
                scrT[gp][:, j * 128:(j + 1) * 128], ALU.add),
                deps=[tt1s[2 * j], tt1s[2 * j + 1], ty1s[2 * j],
                      ty1s[2 * j + 1]], drain=False)
            tyb_list.append(tyb)
        gb[("scrT", gp)] = tyb_list[-1]
        tsqs = []
        for j in range(8):
            tsq = S.op("scalar", lambda e, gp=gp, j=j: e.activation(
                scrSq, ya32[gp][:, j * 128:(j + 1) * 128], AF.Square,
                accum_out=sqac),
                deps=[tyb_list[j], gb.get("sqac_rd")], drain=False)
            tacc = S.op("vector", lambda e, c=c: e.tensor_tensor(
                ssq_sb[:, c:c + 1], ssq_sb[:, c:c + 1], sqac, ALU.add),
                deps=[tsq, t_ssq0], drain=False)
            gb["sqac_rd"] = tacc
            hb = g * 8 + j
            tz = S.op("vector", lambda e, gp=gp, j=j, hb=hb: e.tensor_tensor(
                zc16[:, hb * 128:(hb + 1) * 128],
                ya32[gp][:, j * 128:(j + 1) * 128],
                sgs[gp][:, j * 128:(j + 1) * 128], ALU.mult),
                deps=[tgs[j], tyb_list[j], gb.get("zc_rd")], drain=False)
            tz_list.append(tz)
        gb[("ya32", gp)] = tz_list[-1]
        gb[("xtm", gp)] = tz_list[-1]
        gb[("sgs", gp)] = tz_list[-1]
        return tz_list, tt1s

    for c in range(CN):
        tlf = idmm(lamc[:, c:c + 1], 6, t_lam)
        tlf2 = S.op("vector", lambda e: e.tensor_copy(lamf, pX[:, 0:128]),
                    deps=[tlf])
        bkg[6] = tlf2
        tgm = S.dma("sync", gmtc, gmtD[c].transpose([1, 0, 2]),
                    deps=[t_gmt_sp, gb.get("gmtc_rd")] + fenceB)
        tcf = S.dma("sync", cfmc, cfmD[c].transpose([1, 0, 2]),
                    deps=[t_cfm_sp, gb.get("cfmc_rd")])
        tsz2 = S.dma("sync", szb, szD[c].transpose([1, 0, 2]),
                     deps=[[t_sz[(hb2, c)] for hb2 in range(HBLK)],
                           gb.get("szb_rd")])
        tb1 = S.op("vector", lambda e: e.tensor_tensor(
            sinc, srun, lamf.unsqueeze(2).broadcast_to([128, 128, 64]),
            ALU.mult), deps=[t_sic, tlf2, gb.get("sinc_rd")])
        tb2 = S.op("vector", lambda e: e.tensor_tensor(
            sinc, sinc, szb, ALU.add), deps=[tsz2])
        gb["szb_rd"] = tb2
        last_tz = None
        last_tt1 = None
        for g in range(G):
            tz_list, tt1s = b_group(c, g, tb2, tcf, tgm)
            last_tz = tz_list
            last_tt1 = tt1s
            gb["gmtc_rd"] = tz_list[-1]
            gb["cfmc_rd"] = tt1s[-1]
            gb["sinc_rd"] = tt1s[-1]
        t_zsp[c] = S.dma("sync", zD[c].transpose([1, 0, 2]), zc16,
                         deps=[gb[("ya32", 0)], gb[("ya32", 1)],
                               gb.get("zc_rd")] +
                              [gb[("xtm", 0)], gb[("xtm", 1)]])
        gb["zc_rd"] = t_zsp[c]
    if "zc0" in tap_outs:
        S.dma("sync", tap_outs["zc0"], zc16, deps=[t_zsp[-1]])
    if "ssq" in tap_outs:
        S.dma("sync", tap_outs["ssq"], ssq_sb, deps=[gb["sqac_rd"]])
    tr1 = S.op("scalar", lambda e: e.activation(
        rstd_sb, ssq_sb, AF.Ln, bias=epsc, scale=1.0 / INTER),
        deps=[gb["sqac_rd"], t_eps])
    t_rstd = S.op("scalar", lambda e: e.activation(
        rstd_sb, rstd_sb, AF.Exp, scale=-0.5), deps=[tr1], hard=[tr1])
    if "rstd" in tap_outs:
        S.dma("sync", tap_outs["rstd"], rstd_sb, deps=[t_rstd])
    A32.pop(); A16.pop()
    if phases < 7:
        return _finish_early([t_zsp, t_rstd])

    fenceC = S.fence()
    A32.push(); A16.push()
    rf = A32.al(TOK)
    fscr = [A32.al(512) for _ in range(2)]
    xrs = [A32.al(128) for _ in range(2)]
    outs = [A32.al(128) for _ in range(2)]
    zhalf = A16.al(HBLK * 512)
    wbC = [A16.al(32 * 128) for _ in range(2)]
    gbufC = [A16.al(512) for _ in range(2)]
    cg = {}
    for c in range(CN):
        tpr = idmm(rstd_sb[:, c:c + 1], 6, [t_rstd])
        te = S.op("vector", lambda e, c=c: e.tensor_copy(
            rf[:, c * 128:(c + 1) * 128], pX[:, 0:128]), deps=[tpr],
            drain=False)
        bkg[6] = te
        cg["rf"] = te
    t_out_dmas = []
    for half in range(2):
        t_zh = []
        for hb in range(HBLK):
            for cc in range(4):
                c = half * 4 + cc
                t_zh.append(S.dmaT(
                    "scalar",
                    zhalf[:, hb * 512 + cc * 128:hb * 512 + cc * 128 + 128],
                    zD[c, hb], deps=[t_zsp[c], cg.get("zh")] + fenceC))
        cg["zh"] = t_zh
        zh_readers = []
        pend_tail = []

        def emit_tail(tl):
            ob, i, tf2 = tl
            fdone = []
            for cc in range(4):
                c = half * 4 + cc
                parx = cc & 1
                bank = 6 + (cc & 1)
                txr = S.dma("sync", xrs[parx],
                            dx[c * 128:(c + 1) * 128, ob * 128:(ob + 1) * 128],
                            deps=[cg.get(("xrs", parx))])
                tpf = S.op("tensor", lambda e, cc=cc, bank=bank, i=i:
                           e.transpose(
                               PBF[bank][:, 0:128],
                               fscr[i][:, cc * 128:(cc + 1) * 128], id32),
                           deps=[tf2, bkg[bank]])
                tadd = S.op("vector", lambda e, parx=parx, bank=bank:
                            e.tensor_tensor(
                                outs[parx], PBF[bank][:, 0:128],
                                xrs[parx], ALU.add),
                            deps=[tpf, txr, cg.get(("outs", parx))],
                            drain=False)
                bkg[bank] = tadd
                cg[("xrs", parx)] = tadd
                tso = S.dma("sync",
                            dout[c * 128:(c + 1) * 128, ob * 128:(ob + 1) * 128],
                            outs[parx], deps=[tadd])
                cg[("outs", parx)] = tso
                t_out_dmas.append(tso)
                fdone.append(tadd)
            cg[("fscr_rd", i)] = [tf2] + fdone

        for ob in range(32):
            i = ob & 1
            p_ = pb[2 + i]
            tmm = None
            for kk in range(2):
                src2 = dwoT[kk * 4096:(kk + 1) * 4096,
                            ob * 128:(ob + 1) * 128].rearrange(
                    "(k d) o -> d k o", d=128)
                tw2 = S.dma("gpsimd", wbC[i], src2,
                            deps=[cg.get(("wbC", i))])
                for chb32 in range(32):
                    chb = kk * 32 + chb32
                    st = (chb == 0)
                    sp = (chb == 63)
                    tmm = S.op("tensor",
                               lambda e, i=i, chb=chb, chb32=chb32, st=st,
                               sp=sp, p_=p_: e.matmul(
                                   p_[:], wbC[i][:, chb32 * 128:(chb32 + 1) * 128],
                                   zhalf[:, chb * 512:chb * 512 + 512],
                                   start=st, stop=sp),
                               deps=[tw2, t_zh, cg.get(("pm", i)), t_rstd])
                cg[("wbC", i)] = tmm
            tgl = S.dma("sync", gbufC[i], gD[ob][:, half * 512:(half + 1) * 512],
                        deps=[t_g_done, cg.get(("gb", i))])
            tf1 = S.op("vector", lambda e, p_=p_, half=half, i=i: e.tensor_tensor(
                fscr[i], p_[:], rf[:, half * 512:(half + 1) * 512], ALU.mult),
                deps=[tmm, cg.get("rf"), cg.get(("fscr_rd", i))], drain=False)
            cg[("pm", i)] = tf1
            tf2 = S.op("gpsimd", lambda e, i=i: e.tensor_tensor(
                fscr[i], fscr[i], gbufC[i], ALU.mult), deps=[tgl, tf1],
                drain=False)
            cg[("gb", i)] = tf2
            pend_tail.append((ob, i, tf2))
            if len(pend_tail) > 1:
                emit_tail(pend_tail.pop(0))
            zh_readers.append(tmm)
        while pend_tail:
            emit_tail(pend_tail.pop(0))
        cg["zh"] = zh_readers
    A32.pop(); A16.pop()
    S.op("sync", lambda e: e.nop(), deps=t_out_dmas)

    S.finalize_and_replay(nc)
    stk.close()
    return nc


def _prep_inputs(inputs):
    f = np.float32
    x = np.ascontiguousarray(np.asarray(inputs["L0_attn_hidden_states"]), dtype=f)
    wpT = np.ascontiguousarray(np.asarray(inputs["L0_attn_in_proj_w"], f).T)
    woT = np.ascontiguousarray(np.asarray(inputs["L0_attn_out_proj_w"], f).T)
    wbT = np.ascontiguousarray(np.asarray(inputs["L0_ffn_W_base"], f).T)
    wrT = np.ascontiguousarray(np.asarray(inputs["L0_ffn_W_router"], f).T)
    laT = np.ascontiguousarray(
        np.asarray(inputs["L0_ffn_lora_A"], f).transpose(1, 0, 2).reshape(DIM, 128))
    lbF = np.ascontiguousarray(np.asarray(inputs["L0_ffn_lora_B"], f).reshape(128, DIM))
    cw = np.ascontiguousarray(np.asarray(inputs["L0_attn_conv_w"], f)[:, 0, :]
                              .reshape(80, 128, 4))
    cbc = np.ascontiguousarray(np.asarray(inputs["L0_attn_conv_b"], f)
                               .reshape(80, 128).T)
    dbc = np.asarray(inputs["L0_attn_dt_bias"], f).reshape(128, 1)
    alc = np.asarray(inputs["L0_attn_A_log"], f).reshape(128, 1)
    dc = np.asarray(inputs["L0_attn_D"], f).reshape(128, 1)
    nwc = np.ascontiguousarray(np.asarray(inputs["L0_attn_norm_w"], f)
                               .reshape(64, 128).T)
    bbc = np.ascontiguousarray(np.asarray(inputs["L0_ffn_b_base"], f)
                               .reshape(32, 128).T)
    brc = np.asarray(inputs["L0_ffn_b_router"], f).reshape(8, 1)
    idb = np.eye(128, dtype=f).astype(ml_dtypes.bfloat16)
    id32 = np.eye(128, dtype=f)
    mut = np.triu(np.ones((128, 128), f), 0)
    invk = np.tile((1.0 / np.arange(1, 9, dtype=f))[None, :], (128, 1))
    common = dict(wpT=wpT, woT=woT, wbT=wbT, wrT=wrT, laT=laT, lbF=lbF, cw=cw,
                  cbc=cbc, dbc=dbc, alc=alc, dc=dc, nwc=nwc, bbc=bbc, brc=brc,
                  idb=idb, id32=id32, mut=mut, invk=invk)
    in_maps = []
    for core in range(8):
        b, q = core // 4, core % 4
        xs = np.ascontiguousarray(x[b, q * TOK:(q + 1) * TOK])
        xh = (np.ascontiguousarray(x[b, q * TOK - 3:q * TOK]) if q > 0
              else np.zeros((3, DIM), f))
        r = q
        m = np.zeros((128, 8), f)
        m[:, 0] = 1.0 if r >= 2 else 0.0
        m[:, 1] = 0.0 if r >= 2 else 1.0
        m[:, 2] = 1.0 if r >= 2 else 0.0
        m[:, 3] = 1.0 if r >= 3 else 0.0
        m[:, 4] = 0.0 if r >= 3 else 1.0
        m[:, 5] = 1.0 if r >= 3 else 0.0
        m[:, 6] = 0.0 if r == 0 else 1.0
        in_maps.append(dict(x=xs, xh=xh, ccm=m, **common))
    return in_maps


def kernel(**inputs):
    in_maps = _prep_inputs(inputs)
    nc = build_core_program(taps=tuple(_TAPS), phases=_PHASES)
    res = run_bass_kernel_spmd(nc, in_maps, core_ids=list(range(8)))
    out = np.zeros((2, 4096, DIM), np.float32)
    for core in range(8):
        b, q = core // 4, core % 4
        if "out" in res.results[core]:
            out[b, q * TOK:(q + 1) * TOK] = res.results[core]["out"]
    if _TAPS:
        kernel._tap_results = res.results
    return out


# version 09
# speedup vs baseline: 1.1804x; 1.1804x over previous
import numpy as np
import ml_dtypes
from contextlib import ExitStack

import concourse.bass as bass
import concourse.mybir as mybir
from concourse.bass_utils import run_bass_kernel_spmd

F32 = mybir.dt.float32
BF16 = mybir.dt.bfloat16
AX = mybir.AxisListType
ALU = mybir.AluOpType
AF = mybir.ActivationFunctionType

H, P, G, N, CS = 128, 64, 8, 128, 128
DIM, INTER = 4096, 8192
CONV_DIM = INTER + 2 * G * N
NPROJ = INTER + CONV_DIM + H
EPS = 1e-5
TOK = 1024
CN = TOK // CS
KD = DIM // 128
OB_GATE0, OB_X0, OB_B0, OB_C0, OB_DT = 0, 64, 128, 136, 144
HBLK = 64

_TAPS = []
_SUB = 99
_PHASES = 99

ENGS = ("tensor", "vector", "scalar", "gpsimd", "sync")


class _Op:
    __slots__ = ("eng", "fn", "deps", "need_inc", "is_dma", "sem", "val",
                 "amt", "pre_drain", "sem_override")

    def __init__(self, eng, fn, deps, is_dma):
        self.eng, self.fn, self.deps, self.is_dma = eng, fn, deps, is_dma
        self.need_inc = is_dma
        self.sem = self.val = None
        self.amt = 16 if is_dma else 1
        self.sem_override = None
        self.pre_drain = (not is_dma) and eng in ("vector", "scalar", "gpsimd")


class Sched:
    def __init__(self):
        self.per_eng = {e: [] for e in ENGS}

    def op(self, eng, fn, deps=(), dma=False, hard=(), drain=None):
        dd = []
        stack = list(deps)
        while stack:
            d = stack.pop()
            if d is None:
                continue
            if isinstance(d, (list, tuple)):
                stack.extend(d)
            else:
                dd.append(d)
        kept = []
        for d in dd:
            if d.eng == eng and not d.is_dma:
                continue
            d.need_inc = True
            kept.append(d)
        for d in hard:
            if d is None:
                continue
            d.need_inc = True
            kept.append(d)
        o = _Op(eng, fn, kept, dma)
        if drain is not None:
            o.pre_drain = drain and not dma
        self.per_eng[eng].append(o)
        return o

    def cc(self, fn, deps=()):
        o = self.op("gpsimd", fn, deps=deps, dma=True)
        o.amt = 1
        o.sem_override = "dC"
        return o

    def fence(self):
        toks = []
        for e in ENGS:
            if self.per_eng[e]:
                o = self.per_eng[e][-1]
                o.need_inc = True
                toks.append(o)
        return toks

    def dma(self, q, out, in_, deps=(), **kw):
        return self.op(q, lambda e: e.dma_start(out=out, in_=in_, **kw),
                       deps=deps, dma=True)

    def dmaT(self, q, out, in_, deps=()):
        return self.op(q, lambda e: e.dma_start_transpose(out, in_),
                       deps=deps, dma=True)

    def finalize_and_replay(self, nc):
        semname = {("tensor", False): "tP", ("vector", False): "tV",
                   ("scalar", False): "tA", ("gpsimd", False): "tG",
                   ("sync", False): "tS",
                   ("gpsimd", True): "dG", ("sync", True): "dS",
                   ("scalar", True): "dA", ("vector", True): "dV",
                   ("tensor", True): "dT"}
        counts = {}
        nops = 0
        for e in ENGS:
            for o in self.per_eng[e]:
                nops += 1
                if o.need_inc:
                    o.sem = o.sem_override or semname[(e, o.is_dma)]
                    counts[o.sem] = counts.get(o.sem, 0) + o.amt
                    o.val = counts[o.sem]
        with ExitStack() as stk:
            sems = {s: stk.enter_context(nc.semaphore(s)) for s in counts}
            blk = stk.enter_context(nc.Block())
            for e in ENGS:
                items = self.per_eng[e]

                def mk(items=items):
                    def body(h):
                        hi = {}
                        prev_compute = False
                        for o in items:
                            if o.pre_drain and prev_compute:
                                h.drain()
                            for d in o.deps:
                                if d.val is not None and d.val > hi.get(d.sem, 0):
                                    h.wait_ge(sems[d.sem], d.val)
                                    hi[d.sem] = d.val
                            inst = o.fn(h)
                            prev_compute = not o.is_dma
                            if o.need_inc:
                                assert inst is not None
                                inst.then_inc(sems[o.sem], o.amt)
                    return body
                getattr(blk, e)(mk())


class Arena:
    def __init__(self, t, ncols, name):
        self.t, self.ncols, self.top, self.name = t, ncols, 0, name
        self.marks = []
        self.peak = 0

    def al(self, cols):
        alloc = (cols + 15) & ~15
        off = self.top
        self.top += alloc
        self.peak = max(self.peak, self.top)
        assert self.top <= self.ncols, \
            f"{self.name} overflow {self.top}>{self.ncols}"
        return self.t[:, off:off + cols]

    def push(self):
        self.marks.append(self.top)

    def pop(self):
        self.top = self.marks.pop()


def build_core_program(taps=(), phases=99, sub=None):
    sub = _SUB if sub is None else sub
    nc = bass.Bass("TRN2", target_bir_lowering=False)

    dx = nc.dram_tensor("x", [TOK, DIM], F32, kind="ExternalInput").ap()
    dxh = nc.dram_tensor("xh", [3, DIM], F32, kind="ExternalInput").ap()
    dwpT = nc.dram_tensor("wpT", [DIM, NPROJ], F32, kind="ExternalInput").ap()
    dwoT = nc.dram_tensor("woT", [INTER, DIM], F32, kind="ExternalInput").ap()
    dwbT = nc.dram_tensor("wbT", [DIM, DIM], F32, kind="ExternalInput").ap()
    dwrT = nc.dram_tensor("wrT", [DIM, 8], F32, kind="ExternalInput").ap()
    dlaT = nc.dram_tensor("laT", [DIM, 128], F32, kind="ExternalInput").ap()
    dlbF = nc.dram_tensor("lbF", [128, DIM], F32, kind="ExternalInput").ap()
    dcw = nc.dram_tensor("cw", [80, 128, 4], F32, kind="ExternalInput").ap()
    dcbc = nc.dram_tensor("cbc", [128, 80], F32, kind="ExternalInput").ap()
    ddbc = nc.dram_tensor("dbc", [128, 1], F32, kind="ExternalInput").ap()
    dalc = nc.dram_tensor("alc", [128, 1], F32, kind="ExternalInput").ap()
    ddc = nc.dram_tensor("dc", [128, 1], F32, kind="ExternalInput").ap()
    dnwc = nc.dram_tensor("nwc", [128, 64], F32, kind="ExternalInput").ap()
    dbbc = nc.dram_tensor("bbc", [128, 32], F32, kind="ExternalInput").ap()
    dbrc = nc.dram_tensor("brc", [8, 1], F32, kind="ExternalInput").ap()
    didb = nc.dram_tensor("idb", [128, 128], BF16, kind="ExternalInput").ap()
    did32 = nc.dram_tensor("id32", [128, 128], F32, kind="ExternalInput").ap()
    dmut = nc.dram_tensor("mut", [128, 128], F32, kind="ExternalInput").ap()
    dinvk = nc.dram_tensor("invk", [128, 8], F32, kind="ExternalInput").ap()
    dccm = nc.dram_tensor("ccm", [128, 8], F32, kind="ExternalInput").ap()
    dout = nc.dram_tensor("out", [TOK, DIM], F32, kind="ExternalOutput").ap()

    sgD = nc.dram_tensor("sgD", [HBLK, CN, 128, 128], BF16).ap()
    xfD = nc.dram_tensor("xfD", [HBLK, CN, 128, 128], BF16).ap()
    zD = nc.dram_tensor("zD", [CN, HBLK, 128, 128], BF16).ap()
    szD = nc.dram_tensor("szD", [CN, HBLK, 128, 128], BF16).ap()
    gD = nc.dram_tensor("gD", [32, 128, TOK], BF16).ap()
    gmtD = nc.dram_tensor("gmtD", [CN, G, 128, 128], BF16).ap()
    cfmD = nc.dram_tensor("cfmD", [CN, G, 128, 128], BF16).ap()
    btmD = nc.dram_tensor("btmD", [G, CN, 128, 128], BF16).ap()
    ccSi = nc.dram_tensor("ccSi", [128, N * P], BF16).ap()
    ccSo = nc.dram_tensor("ccSo", [512, N * P], BF16).ap()
    ccdi = nc.dram_tensor("ccdi", [128, 1], F32).ap()
    ccdo = nc.dram_tensor("ccdo", [512, 1], F32).ap()

    tap_shapes = {"xn0": ([128, DIM], F32), "dtfm": ([128, TOK], F32),
                  "csT": ([128, TOK], F32), "expcsT": ([128, TOK], F32),
                  "dtT": ([128, TOK], F32), "dtdecT": ([128, TOK], F32),
                  "lam": ([128, CN], F32), "dcol": ([128, 1], F32),
                  "decall": ([128, TOK], F32), "cfm": ([128, G * TOK], BF16),
                  "gmt0": ([128, 128], BF16), "btm0": ([128, 128], BF16),
                  "srun": ([128, N * P], F32), "sincore": ([128, N * P], F32),
                  "ssq": ([128, CN], F32), "rstd": ([128, CN], F32),
                  "hff": ([128, TOK], F32), "wT0": ([128, 8], F32),
                  "logit": ([8, TOK], F32), "zc0": ([128, HBLK * 128], BF16),
                  "lasb": ([128, KD * 128], BF16)}
    tap_outs = {t: nc.dram_tensor("tap_" + t, tap_shapes[t][0],
                                  tap_shapes[t][1], kind="ExternalOutput").ap()
                for t in taps}

    S = Sched()
    stk = ExitStack()
    A32 = Arena(stk.enter_context(nc.sbuf_tensor("AR32", [128, 24100], F32)),
                24100, "A32")
    A16 = Arena(stk.enter_context(nc.sbuf_tensor("AR16", [128, 54200], BF16)),
                54200, "A16")
    pb = [stk.enter_context(nc.psum_tensor(f"pb{i}", [128, 512], F32))
          for i in range(8)]
    pX = pb[6]
    pXb = pX.ap().bitcast(BF16)
    pS_ = pb[7]
    pSb = pS_.ap().bitcast(BF16)

    id32 = A32.al(128); mut = A32.al(128); invk = A32.al(8); ccm = A32.al(8)
    cbc = A32.al(80); dbc = A32.al(1); alc = A32.al(1); acol = A32.al(1)
    dc_ = A32.al(1); nwc = A32.al(64); bbc = A32.al(32)
    brc_t = stk.enter_context(nc.sbuf_tensor("brcs", [8, 16], F32))
    brc = brc_t[:, 0:1]
    ones = A32.al(1027)
    epsc = A32.al(1)
    cwb = A32.al(8)
    csT = A32.al(TOK); expcsT = A32.al(TOK); dtT = A32.al(TOK)
    lamc = A32.al(CN); dcolc = A32.al(1); dful = A32.al(128)
    ssq_sb = A32.al(CN); rstd_sb = A32.al(CN)
    tcum = A32.al(CN); tcum2 = A32.al(CN); etc_ = A32.al(CN)
    bigr = A32.al(N * P)
    srun = bigr
    idb = A16.al(128)
    cshiT = A16.al(TOK); csloT = A16.al(TOK)

    t_consts = []
    for dst, src in ((idb, didb), (id32, did32), (mut, dmut), (invk, dinvk),
                     (ccm, dccm), (cbc, dcbc), (dbc, ddbc), (alc, dalc),
                     (dc_, ddc), (nwc, dnwc), (bbc, dbbc), (brc, dbrc)):
        t_consts.append(S.dma("sync", dst, src))
    tc_idb, tc_id32, tc_mut, tc_invk, tc_ccm, tc_cbc, tc_dbc, tc_alc, \
        tc_dc, tc_nwc, tc_bbc, tc_brc = t_consts
    t_ones = S.op("vector", lambda e: e.memset(ones, 1.0))
    t_eps = S.op("vector", lambda e: e.memset(epsc, EPS))


    def _finish_early(deps):
        toks = []
        for c in range(CN):
            toks.append(S.dma("sync", dout[c * 128:(c + 1) * 128, :],
                              bigr[:, 0:DIM],
                              deps=deps + [t_ones] + toks[-1:]))
        S.op("sync", lambda e: e.nop(), deps=toks)
        S.finalize_and_replay(nc)
        stk.close()
        return nc

    px_guard = {}
    bkg = {6: None, 7: None}
    PBF = {6: pX, 7: pS_}
    PBB = {6: pXb, 7: pSb}

    def idmm(col_ap, bank, deps):
        return S.op("tensor", lambda e, bank=bank: e.matmul(
            PBF[bank][:, 0:128], col_ap.broadcast_to([128, 128]), id32,
            start=True, stop=True),
            deps=list(deps) + [tc_id32, bkg[bank]])

    A32.push(); A16.push()
    xnT = A16.al(KD * 1027)
    wbA = [A16.al(KD * 128) for _ in range(2)]
    dtfm = A32.al(TOK)
    dtdecT = A32.al(TOK)
    decall = A32.al(CN * 128)
    scr1 = A32.al(TOK); scr2 = A32.al(TOK)
    cbuf = A32.al(1027)

    A16.push()
    xnbf = A16.al(DIM)
    xbuf = bigr[:, 0:DIM]
    lnm = A32.al(1); lnv = A32.al(1); lnq = A32.al(1)
    t_xn_parts = []
    prev_txn = None
    prev_tl_reader = None
    for c in range(CN + 1):
        halo = (c == CN)
        rows = 3 if halo else 128
        src = dxh if halo else dx[c * 128:(c + 1) * 128, :]
        tl = S.dma("sync", xbuf[0:rows, :], src, deps=[prev_tl_reader])
        tm0 = S.op("vector", lambda e, r=rows: e.tensor_reduce(
            lnm[0:r, :], xbuf[0:r, :], AX.X, ALU.add), deps=[tl])
        tq = S.op("scalar", lambda e, r=rows: e.activation(
            xnbf[0:r, :], xbuf[0:r, :], AF.Square, accum_out=lnv[0:r, :]),
            deps=[tl, prev_txn])
        S.op("vector", lambda e, r=rows: e.tensor_scalar(
            lnm[0:r, :], lnm[0:r, :], 1.0 / DIM, None, ALU.mult))
        S.op("vector", lambda e, r=rows: e.tensor_tensor(
            lnq[0:r, :], lnm[0:r, :], lnm[0:r, :], ALU.mult))
        tv = S.op("vector", lambda e, r=rows: e.tensor_scalar(
            lnv[0:r, :], lnv[0:r, :], 1.0 / DIM, None, ALU.mult), deps=[tq])
        S.op("vector", lambda e, r=rows: e.tensor_tensor(
            lnv[0:r, :], lnv[0:r, :], lnq[0:r, :], ALU.subtract))
        tvar = S.op("vector", lambda e, r=rows: e.engine_nop()) if False else None
        tr = S.op("scalar", lambda e, r=rows: e.activation(
            lnv[0:r, :], lnv[0:r, :], AF.Ln, bias=epsc[0:r, :]),
            deps=[tv, t_eps])
        tr2 = S.op("scalar", lambda e, r=rows: e.activation(
            lnv[0:r, :], lnv[0:r, :], AF.Exp, scale=-0.5), hard=[tr])
        txn = S.op("vector", lambda e, r=rows: e.tensor_scalar(
            xnbf[0:r, :], xbuf[0:r, :], lnm[0:r, :], lnv[0:r, :],
            ALU.subtract, ALU.mult), deps=[tr2, tq])
        prev_txn_parts = []
        if "xn0" in tap_outs and c == 0:
            S.dma("gpsimd", tap_outs["xn0"], xnbf, deps=[txn])
        for k in range(KD):
            col = k * 1027 + (0 if halo else 3 + c * 128)
            w = 3 if halo else 128
            bank = 6 + (k & 1)
            tp = S.op("tensor", lambda e, k=k, r=rows, bank=bank: e.transpose(
                PBB[bank][0:128, 0:r],
                xnbf[0:r, k * 128:(k + 1) * 128], idb[0:r, 0:r]),
                deps=[txn, tc_idb, bkg[bank]])
            te = S.op("vector", lambda e, col=col, w=w, bank=bank: e.tensor_copy(
                xnT[:, col:col + w], PBB[bank][0:128, 0:w]),
                deps=[tp])
            bkg[bank] = te
            prev_txn_parts.append(te)
        prev_txn = prev_txn_parts
        prev_tl_reader = txn
        t_xn_parts.extend(prev_txn_parts)
    t_xn = t_xn_parts[-1]
    A16.pop()

    if phases < 1:
        return _finish_early([t_xn])

    wb_guard = [None, None]
    main_guard = [None, None]
    tail_guard = [None, None]
    sstate = {"i": 0}

    def stream_ob(dramW, ob, conv, deps=(), stop_last=True):
        i = sstate["i"] & 1
        sstate["i"] += 1
        src = dramW[:, ob * 128:(ob + 1) * 128].rearrange(
            "(k d) o -> d k o", d=128)
        tw = S.dma("gpsimd", wbA[i][:, 0:KD * 128], src,
                   deps=[wb_guard[i]] + list(deps))
        pA_, pB_, pT_ = pb[2 * i], pb[2 * i + 1], pb[4 + i]
        t_mm = None
        for k in range(KD):
            lhs = wbA[i][:, k * 128:(k + 1) * 128]
            st = (k == 0)
            sp = (k == KD - 1) and stop_last
            base = k * 1027
            if conv:
                S.op("tensor", lambda e, lhs=lhs, st=st, sp=sp, base=base,
                     pA_=pA_: e.matmul(pA_[:], lhs, xnT[:, base:base + 512],
                                       start=st, stop=sp),
                     deps=[tw, t_xn, main_guard[i], tail_guard[i]])
                S.op("tensor", lambda e, lhs=lhs, st=st, sp=sp, base=base,
                     pB_=pB_: e.matmul(pB_[:], lhs,
                                       xnT[:, base + 512:base + 1024],
                                       start=st, stop=sp))
                t_mm = S.op("tensor", lambda e, lhs=lhs, st=st, sp=sp,
                            base=base, pT_=pT_: e.matmul(
                                pT_[:, 0:3], lhs,
                                xnT[:, base + 1024:base + 1027],
                                start=st, stop=sp))
            else:
                S.op("tensor", lambda e, lhs=lhs, st=st, sp=sp, base=base,
                     pA_=pA_: e.matmul(pA_[:], lhs, xnT[:, base + 3:base + 515],
                                       start=st, stop=sp),
                     deps=[tw, t_xn, main_guard[i]])
                t_mm = S.op("tensor", lambda e, lhs=lhs, st=st, sp=sp,
                            base=base, pB_=pB_: e.matmul(
                                pB_[:], lhs, xnT[:, base + 515:base + 1027],
                                start=st, stop=sp))
        wb_guard[i] = t_mm
        return i, t_mm

    def conv_silu(i, t_mm, co, dst16, deps=()):
        pA_, pB_, pT_ = pb[2 * i], pb[2 * i + 1], pb[4 + i]
        cw_sl = cwb[:, (co & 1) * 4:(co & 1) * 4 + 4]
        tcw = S.dma("sync", cw_sl, dcw[co], deps=[sstate.get(("cw", co & 1))])
        e0 = S.op("vector", lambda e: e.tensor_copy(cbuf[:, 0:512], pA_[:]),
                  deps=[t_mm, sstate.get("cbuf_g")], drain=False)
        e1 = S.op("vector", lambda e: e.tensor_copy(cbuf[:, 512:1024], pB_[:]),
                  drain=False)
        e2 = S.op("vector", lambda e: e.tensor_copy(cbuf[:, 1024:1027],
                                                    pT_[:, 0:3]), drain=False)
        tc0 = S.op("vector", lambda e: e.tensor_scalar(
            scr1[:, 0:1024], cbuf[:, 0:1024], cw_sl[:, 0:1], cbc[:, co:co + 1],
            ALU.mult, ALU.add),
            deps=[e0, e1, e2, tcw, tc_cbc, sstate.get("scr1_g")])
        for k in range(1, 4):
            tc0 = S.op("vector", lambda e, k=k: e.scalar_tensor_tensor(
                scr1[:, 0:1024], cbuf[:, k:k + 1024], cw_sl[:, k:k + 1],
                scr1[:, 0:1024], ALU.mult, ALU.add))
        sstate["cbuf_g"] = tc0
        sstate[("cw", co & 1)] = tc0
        tsl = S.op("scalar", lambda e: e.activation(
            dst16, scr1[:, 0:1024], AF.Silu), deps=[tc0] + list(deps))
        sstate["scr1_g"] = tsl
        return tsl, [e0, e1, e2]

    i, t_mm = stream_ob(dwpT, OB_DT, False)
    pA_, pB_ = pb[2 * i], pb[2 * i + 1]
    if phases == 1 and sub <= -1:
        tev0 = S.op("vector", lambda e: e.tensor_copy(dtfm[:, 0:512], pA_[:]),
                    deps=[t_mm])
        tev1 = S.op("vector", lambda e: e.tensor_copy(dtfm[:, 512:1024], pB_[:]))
        return _finish_early([tev1])
    dt_ev = []
    for half, p_ in ((0, pA_), (1, pB_)):
        sl = slice(half * 512, (half + 1) * 512)
        ta_ = S.op("scalar", lambda e, p_=p_, sl=sl: e.activation(
            scr1[:, sl], p_[:], AF.Abs, bias=dbc), deps=[t_mm, tc_dbc])
        _h = ta_
        if sub == 10:
            dt_ev.append(S.op("vector", lambda e, sl=sl: e.tensor_copy(
                dtfm[:, sl], scr1[:, sl]), deps=[ta_]))
            continue
        tb_ = S.op("scalar", lambda e, sl=sl: e.activation(
            scr1[:, sl], scr1[:, sl], AF.Exp, scale=-1.0), hard=[_h])
        tc2 = S.op("scalar", lambda e, sl=sl: e.activation(
            scr1[:, sl], scr1[:, sl], AF.Ln, bias=ones[:, 0:1]),
            deps=[t_ones], hard=[tb_])
        if sub == 11:
            dt_ev.append(S.op("vector", lambda e, sl=sl: e.tensor_copy(
                dtfm[:, sl], scr1[:, sl]), deps=[tc2]))
            continue
        S.op("vector", lambda e, p_=p_, sl=sl: e.tensor_scalar(
            scr2[:, sl], p_[:], dbc, 0.0, ALU.add, ALU.max),
            deps=[t_mm, tc_dbc, ta_])
        dt_ev.append(S.op("vector", lambda e, sl=sl: e.tensor_tensor(
            dtfm[:, sl], scr1[:, sl], scr2[:, sl], ALU.add), deps=[tc2]))
    main_guard[i] = dt_ev
    t_dt = dt_ev[-1]
    sstate["scr1_g"] = t_dt
    if "dtfm" in tap_outs:
        S.dma("sync", tap_outs["dtfm"], dtfm, deps=[t_dt])
    if phases == 1 and sub <= 0 or sub in (10, 11):
        return _finish_early([t_dt])

    t1 = S.op("scalar", lambda e: e.activation(acol, alc, AF.Exp),
              deps=[tc_alc])
    t_a = S.op("vector", lambda e: e.tensor_scalar(
        acol, acol, -1.0, None, ALU.mult), deps=[t1])
    t_adt = S.op("vector", lambda e: e.tensor_scalar(
        scr1[:, 0:TOK], dtfm, acol, None, ALU.mult), deps=[t_a, t_dt])
    csfm = scr2
    tcs = []
    for c in range(CN):
        sl = slice(c * 128, (c + 1) * 128)
        tcs.append(S.op("vector", lambda e, sl=sl: e.tensor_tensor_scan(
            csfm[:, sl], ones[:, 0:128], scr1[:, sl], 0.0, ALU.mult, ALU.add),
            deps=[t_adt, t_ones]))
    sstate["scr1_g"] = tcs[-1]
    if phases == 1 and sub <= 1:
        return _finish_early([tcs[-1]])
    t_csT = []
    for c in range(CN):
        sl = slice(c * 128, (c + 1) * 128)
        tp = S.op("tensor", lambda e, sl=sl: e.transpose(
            pX[:, 0:128], csfm[:, sl], id32),
            deps=[tcs[c], tc_id32, bkg[6]])
        te = S.op("vector", lambda e, sl=sl: e.tensor_copy(
            csT[:, sl], pX[:, 0:128]), deps=[tp])
        bkg[6] = te
        t_csT.append(te)
        tp2 = S.op("tensor", lambda e, sl=sl: e.transpose(
            pS_[:, 0:128], dtfm[:, sl], id32),
            deps=[t_dt, bkg[7]])
        te2 = S.op("vector", lambda e, sl=sl: e.tensor_copy(
            dtT[:, sl], pS_[:, 0:128]), deps=[tp2])
        bkg[7] = te2
    t_hilo = []
    for c in range(CN):
        sl = slice(c * 128, (c + 1) * 128)
        S.op("vector", lambda e, sl=sl: e.tensor_copy(cshiT[:, sl], csT[:, sl]),
             deps=[t_csT[c]])
        S.op("vector", lambda e, sl=sl: e.tensor_tensor(
            scr1[:, sl], csT[:, sl], cshiT[:, sl], ALU.subtract))
        t_hilo.append(S.op("vector", lambda e, sl=sl: e.tensor_copy(
            csloT[:, sl], scr1[:, sl])))
    sstate["scr1_g"] = t_hilo[-1]
    t_expcs = S.op("scalar", lambda e: e.activation(expcsT, csT, AF.Exp),
                   deps=[t_csT[-1]])
    if phases == 1 and sub <= 2:
        return _finish_early([t_expcs, t_hilo])
    t_tc = S.op("vector", lambda e: e.tensor_copy(tcum, csfm[:, 127::128]),
                deps=tcs)
    t_tcum = S.op("vector", lambda e: e.tensor_tensor_scan(
        tcum2, ones[:, 0:CN], tcum, 0.0, ALU.mult, ALU.add), deps=[t_tc])
    t_dcol = S.op("scalar", lambda e: e.activation(
        dcolc, tcum2[:, CN - 1:CN], AF.Exp), deps=[t_tcum])
    t_lam1 = S.op("scalar", lambda e: e.activation(
        lamc[:, 1:CN], tcum2[:, 0:CN - 1], AF.Exp), deps=[t_tcum])
    t_lam0 = S.op("vector", lambda e: e.memset(lamc[:, 0:1], 1.0))
    t_lam = [t_lam0, t_lam1]
    t_etc = S.op("scalar", lambda e: e.activation(
        etc_, csfm[:, 127::128], AF.Exp), deps=tcs)
    if phases == 1 and sub <= 3:
        return _finish_early([t_dcol, t_lam, t_etc])
    t_decall = []
    for c in range(CN):
        tp = idmm(etc_[:, c:c + 1], 6, [t_etc])
        te = S.op("vector", lambda e, c=c: e.tensor_copy(
            decall[:, c * 128:(c + 1) * 128], pX[:, 0:128]),
            deps=[tp])
        bkg[6] = te
        t_decall.append(te)
        sl = slice(c * 128, (c + 1) * 128)
        tp2 = idmm(csfm[:, c * 128 + 127:c * 128 + 128], 7, [tcs[c]])
        tb = S.op("vector", lambda e, sl=sl: e.tensor_tensor(
            scr1[:, sl], pS_[:, 0:128], csT[:, sl],
            ALU.subtract), deps=[tp2, t_csT[c], t_hilo])
        bkg[7] = tb
        tc3 = S.op("scalar", lambda e, sl=sl: e.activation(
            scr1[:, sl], scr1[:, sl], AF.Exp), deps=[tb])
        t_dtdec = S.op("vector", lambda e, sl=sl: e.tensor_tensor(
            dtdecT[:, sl], dtT[:, sl], scr1[:, sl], ALU.mult), deps=[tc3])
    sstate["scr1_g"] = t_dtdec
    tpD = idmm(dc_, 6, [tc_dc])
    t_dful = S.op("vector", lambda e: e.tensor_copy(dful, pX[:, 0:128]),
                  deps=[tpD])
    bkg[6] = t_dful
    for nm, buf, dep in (("csT", csT, t_csT), ("expcsT", expcsT, [t_expcs]),
                         ("dtT", dtT, t_csT), ("dtdecT", dtdecT, [t_dtdec]),
                         ("decall", decall, t_decall), ("lam", lamc, t_lam),
                         ("dcol", dcolc, [t_dcol])):
        if nm in tap_outs:
            S.dma("sync", tap_outs[nm], buf, deps=list(dep) + [t_dtdec])
    if phases < 2:
        return _finish_early([t_dtdec, t_dful, t_expcs])

    A16.push()
    cfm = A16.al(G * TOK)
    bfmg = A16.al(TOK)
    btmg = A16.al(CN * 128)
    gmtb = [A16.al(128) for _ in range(2)]
    t_cfm = [None] * G
    t_cfm_sp = [None] * G
    for g in range(G):
        i, t_mm = stream_ob(dwpT, OB_C0 + g, True)
        dst = cfm[:, g * TOK:(g + 1) * TOK]
        tsl, ev = conv_silu(i, t_mm, 72 + g, dst,
                            deps=[sstate.get(("cfm_g", g))])
        main_guard[i] = ev[1:3]
        tail_guard[i] = ev[0]
        t_cfm[g] = tsl
        t_cfm_sp[g] = S.dma("sync", cfmD[:, g].transpose([1, 0, 2]), dst,
                            deps=[tsl])
    if "cfm" in tap_outs:
        S.dma("sync", tap_outs["cfm"], cfm, deps=t_cfm)
    gm_guard = {}
    t_btm_sp = []
    t_gmt_sp = []
    for g in range(G):
        i, t_mm = stream_ob(dwpT, OB_B0 + g, True)
        tsl, ev = conv_silu(i, t_mm, 64 + g, bfmg,
                            deps=[sstate.get("bfm_g")])
        main_guard[i] = ev[1:3]
        tail_guard[i] = ev[0]
        bconsumers = []
        te2 = None
        for c in range(CN):
            tg = S.op("tensor", lambda e, g=g, c=c: e.matmul(
                pX[:, 0:128],
                bfmg[:, c * 128:(c + 1) * 128],
                cfm[:, g * TOK + c * 128:g * TOK + (c + 1) * 128],
                start=True, stop=True),
                deps=[tsl, t_cfm[g], bkg[6]])
            par = c & 1
            te = S.op("vector", lambda e, par=par: e.tensor_tensor(
                gmtb[par], pX[:, 0:128], mut, ALU.mult),
                deps=[tg, tc_mut, gm_guard.get(("gm", par))])
            bkg[6] = te
            tgs = S.dma("sync", gmtD[c, g], gmtb[par], deps=[te])
            gm_guard[("gm", par)] = tgs
            t_gmt_sp.append(tgs)
            bconsumers.append(tg)
            if "gmt0" in tap_outs and g == 0 and c == 0:
                S.dma("sync", tap_outs["gmt0"], gmtb[par], deps=[te])
            tb2 = S.op("tensor", lambda e, c=c: e.transpose(
                pSb[:, 0:128], bfmg[:, c * 128:(c + 1) * 128], idb),
                deps=[tsl, bkg[7]])
            te2 = S.op("vector", lambda e, c=c: e.tensor_copy(
                btmg[:, c * 128:(c + 1) * 128], pSb[:, 0:128]), deps=[tb2])
            bkg[7] = te2
            bconsumers.append(tb2)
        tbs = S.dma("sync", btmD[g], btmg, deps=[te2, gm_guard.get("btm_sp")])
        gm_guard["btm_sp"] = tbs
        t_btm_sp.append(tbs)
        if "btm0" in tap_outs and g == 0:
            S.dma("sync", tap_outs["btm0"], btmg[:, 0:128], deps=[te2])
        sstate["bfm_g"] = bconsumers
    A16.pop()
    if phases < 3:
        return _finish_early([t_btm_sp, t_gmt_sp, t_cfm_sp])

    fenceA2 = S.fence()
    A32.push(); A16.push()
    logit = A32.al(TOK)
    zT0 = A32.al(CN * 8)
    zA = A32.al(8); zB = A32.al(8); zC = A32.al(8)
    tauc = A32.al(1)
    hs32 = A32.al(TOK)
    sgbuf = A32.al(TOK)
    wr_sb = A16.al(KD * 8)
    wTb = A16.al(CN * 8)
    hwb = A16.al(TOK)
    gbuf16 = A16.al(TOK)
    sgn16 = A16.al(TOK)

    t_wr = S.dma("gpsimd", wr_sb, dwrT.rearrange("(k d) o -> d k o", d=128),
                 deps=fenceA2)
    t_log_ev = []
    for half in range(2):
        tm2 = None
        for k in range(KD):
            st, sp = (k == 0), (k == KD - 1)
            base = k * 1027 + 3 + half * 512
            tm2 = S.op("tensor", lambda e, k=k, st=st, sp=sp, base=base:
                       e.matmul(pS_[0:8, 0:512], wr_sb[:, k * 8:(k + 1) * 8],
                                xnT[:, base:base + 512], start=st, stop=sp),
                       deps=[t_wr, t_xn, t_log_ev, bkg[7]])
        te = S.op("scalar", lambda e, half=half: e.activation(
            logit[0:8, half * 512:(half + 1) * 512], pS_[0:8, 0:512],
            AF.Identity, bias=brc), deps=[tm2, tc_brc])
        bkg[7] = te
        t_log_ev.append(te)
    if "logit" in tap_outs:
        S.dma("sync", tap_outs["logit"], logit[0:8, :], deps=t_log_ev)
    SORT_STAGES = ["c", "b", "s3", "a", "s5", "s6"]
    t_w = [None] * CN
    pz_guard = [None]
    for c in range(CN):
        tpz = S.op("tensor", lambda e, c=c: e.transpose(
            pS_[:, 0:8], logit[0:8, c * 128:(c + 1) * 128], id32[0:8, 0:8]),
            deps=[t_log_ev, bkg[7]])
        tz0 = S.op("vector", lambda e, c=c: e.tensor_copy(
            zT0[:, c * 8:(c + 1) * 8], pS_[:, 0:8]), deps=[tpz])
        bkg[7] = tz0
        pz_guard[0] = tz0
        S.op("vector", lambda e, c=c: e.tensor_copy(
            zA[:, 0:8], zT0[:, c * 8:(c + 1) * 8]))
        bufs = [zA, zB, zC]
        cur = 0

        def v4(t):
            return t[:, 0:8].rearrange("p (a b c2) -> p a b c2", b=2, c2=2)
        for kind in SORT_STAGES:
            ca, na = v4(bufs[cur]), v4(bufs[(cur + 1) % 3])
            cu, nx = bufs[cur], bufs[(cur + 1) % 3]
            if kind == "c":
                lo, hi = ca[:, :, :, 0:1], ca[:, :, :, 1:2]
                lod, hid = na[:, :, :, 0:1], na[:, :, :, 1:2]
                uns = []
            elif kind == "b":
                lo, hi = ca[:, :, 0:1, :], ca[:, :, 1:2, :]
                lod, hid = na[:, :, 0:1, :], na[:, :, 1:2, :]
                uns = []
            elif kind == "s3":
                lo, hi = ca[:, :, 0:1, 1:2], ca[:, :, 1:2, 0:1]
                lod, hid = na[:, :, 0:1, 1:2], na[:, :, 1:2, 0:1]
                uns = [(ca[:, :, 0:1, 0:1], na[:, :, 0:1, 0:1]),
                       (ca[:, :, 1:2, 1:2], na[:, :, 1:2, 1:2])]
            elif kind == "a":
                lo, hi = ca[:, 0:1, :, :], ca[:, 1:2, :, :]
                lod, hid = na[:, 0:1, :, :], na[:, 1:2, :, :]
                uns = []
            elif kind == "s5":
                lo, hi = ca[:, 0:1, 1:2, :], ca[:, 1:2, 0:1, :]
                lod, hid = na[:, 0:1, 1:2, :], na[:, 1:2, 0:1, :]
                uns = [(ca[:, 0:1, 0:1, :], na[:, 0:1, 0:1, :]),
                       (ca[:, 1:2, 1:2, :], na[:, 1:2, 1:2, :])]
            else:
                lo, hi = cu[:, 1:7:2], cu[:, 2:8:2]
                lod, hid = nx[:, 1:7:2], nx[:, 2:8:2]
                uns = [(cu[:, 0:1], nx[:, 0:1]), (cu[:, 7:8], nx[:, 7:8])]
            S.op("vector", lambda e, lo=lo, hi=hi, lod=lod: e.tensor_tensor(
                lod, lo, hi, ALU.max))
            S.op("vector", lambda e, lo=lo, hi=hi, hid=hid: e.tensor_tensor(
                hid, lo, hi, ALU.min))
            for (ua, ub) in uns:
                S.op("vector", lambda e, ua=ua, ub=ub: e.tensor_copy(ub, ua))
            cur = (cur + 1) % 3
        cu, nx = bufs[cur], bufs[(cur + 1) % 3]
        S.op("vector", lambda e, cu=cu, nx=nx: e.tensor_tensor_scan(
            nx[:, 0:8], ones[:, 0:8], cu[:, 0:8], 0.0, ALU.mult, ALU.add))
        S.op("vector", lambda e, nx=nx: e.tensor_scalar(
            nx[:, 0:8], nx[:, 0:8], -1.0, None, ALU.add))
        S.op("vector", lambda e, nx=nx: e.tensor_tensor(
            nx[:, 0:8], nx[:, 0:8], invk, ALU.mult), deps=[tc_invk])
        S.op("vector", lambda e, nx=nx: e.tensor_reduce(
            tauc, nx[:, 0:8], AX.X, ALU.max))
        t_w[c] = S.op("vector", lambda e, c=c: e.tensor_scalar(
            wTb[:, c * 8:(c + 1) * 8], zT0[:, c * 8:(c + 1) * 8], tauc, 0.0,
            ALU.subtract, ALU.max))
    if "wT0" in tap_outs:
        S.dma("gpsimd", tap_outs["wT0"], wTb[:, 0:8], deps=[t_w[0]])
    A16.push()
    la_sb = A16.al(KD * 128)
    t_la = S.dma("gpsimd", la_sb, dlaT.rearrange("(k d) o -> d k o", d=128),
                 deps=fenceA2)
    t_h_ev = []
    for half in range(2):
        tm2 = None
        for k in range(KD):
            st, sp = (k == 0), (k == KD - 1)
            base = k * 1027 + 3 + half * 512
            tm2 = S.op("tensor", lambda e, k=k, st=st, sp=sp, base=base:
                       e.matmul(pX[:], la_sb[:, k * 128:(k + 1) * 128],
                                xnT[:, base:base + 512], start=st, stop=sp),
                       deps=[t_la, t_xn, t_h_ev, bkg[6]])
        teh = S.op("vector", lambda e, half=half: e.tensor_copy(
            hs32[:, half * 512:(half + 1) * 512], pX[:]), deps=[tm2],
            drain=False)
        bkg[6] = teh
        t_h_ev.append(teh)
    if "hff" in tap_outs:
        S.dma("sync", tap_outs["hff"], hs32, deps=t_h_ev)
    if "lasb" in tap_outs:
        S.dma("sync", tap_outs["lasb"], la_sb, deps=[t_la] + t_h_ev)
    t_hw = []
    wrep_l = A16.al(128)
    for c in range(CN):
        twc = S.op("vector", lambda e, c=c: e.tensor_copy(
            wrep_l,
            wTb[:, c * 8:(c + 1) * 8].unsqueeze(2).broadcast_to([128, 8, 16])),
            deps=[t_w[c], sstate.get("wrep_g")])
        twr2 = S.op("tensor", lambda e, c=c: e.transpose(
            pSb[:, 0:128], wrep_l, idb),
            deps=[twc, tc_idb, bkg[7]])
        sstate["wrep_g"] = twr2
        th2 = S.op("vector", lambda e, c=c: e.tensor_tensor(
            hwb[:, c * 128:(c + 1) * 128], hs32[:, c * 128:(c + 1) * 128],
            pSb[:, 0:128], ALU.mult), deps=[twr2, t_h_ev], drain=False)
        bkg[7] = th2
        t_hw.append(th2)
    A16.pop()
    A16.push()
    lb_sb = A16.al(32 * 128)
    t_lb = S.dma("gpsimd", lb_sb, dlbF, deps=S.fence())
    g_guard = [None]
    for ob in range(32):
        i, t_mm = stream_ob(dwbT, ob, False, stop_last=False)
        pA_, pB_ = pb[2 * i], pb[2 * i + 1]
        ev = []
        for half, p_ in ((0, pA_), (1, pB_)):
            tl2 = S.op("tensor", lambda e, ob=ob, half=half, p_=p_: e.matmul(
                p_[:], lb_sb[:, ob * 128:(ob + 1) * 128],
                hwb[:, half * 512:(half + 1) * 512], start=False, stop=True),
                deps=[t_lb] + t_hw)
            ev.append(S.op("scalar", lambda e, ob=ob, half=half, p_=p_:
                           e.activation(gbuf16[:, half * 512:(half + 1) * 512],
                                        p_[:], AF.Silu, bias=bbc[:, ob:ob + 1]),
                           deps=[tl2, tc_bbc, g_guard[0]]))
        wb_guard[i] = tl2
        main_guard[i] = ev
        tg2 = S.dma("sync", gD[ob], gbuf16, deps=ev)
        g_guard[0] = tg2
    t_g_done = g_guard[0]
    sg_guard = [None]
    for hb in range(HBLK):
        i, t_mm = stream_ob(dwpT, OB_GATE0 + hb, False)
        pA_, pB_ = pb[2 * i], pb[2 * i + 1]
        ev = []
        for half, p_ in ((0, pA_), (1, pB_)):
            ev.append(S.op("scalar", lambda e, p_=p_, half=half: e.activation(
                sgbuf[:, half * 512:(half + 1) * 512], p_[:], AF.Silu),
                deps=[t_mm, sg_guard[0]]))
        tn = S.op("vector", lambda e, hb=hb: e.tensor_scalar(
            sgn16, sgbuf, nwc[:, hb:hb + 1], None, ALU.mult),
            deps=ev + [tc_nwc, sg_guard[0]], drain=False)
        main_guard[i] = tn
        tsp = S.dma("sync", sgD[hb].transpose([1, 0, 2]), sgn16, deps=[tn])
        sg_guard[0] = tsp
    t_sg_done = sg_guard[0]
    A16.pop()
    A32.pop(); A16.pop()
    if phases < 4:
        return _finish_early([t_g_done, t_sg_done])

    fenceA3 = S.fence()
    A16.push()
    xf16 = A16.al(TOK)
    btmg2 = A16.al(CN * 128)
    xs2b = A16.al(128)
    szst = [A16.al(128) for _ in range(2)]
    t_zero_srun = S.op("vector", lambda e: e.memset(srun, 0.0),
                       deps=[t_xn, t_dt])
    t_sz = {}
    a3 = {("pst", 0): t_hw, ("pst", 1): t_hw}
    t_btm_ld = None
    xf_guard = [None]
    for hb in range(HBLK):
        g = hb // 8
        if hb % 8 == 0:
            t_btm_ld = S.dma("sync", btmg2, btmD[g],
                             deps=[t_btm_sp[g], a3.get("btm_rd")] + fenceA3)
        i, t_mm = stream_ob(dwpT, OB_X0 + hb, True)
        tsl, ev = conv_silu(i, t_mm, hb, xf16, deps=[xf_guard[0]])
        main_guard[i] = ev[1:3]
        tail_guard[i] = ev[0]
        txf = S.dma("sync", xfD[hb].transpose([1, 0, 2]), xf16, deps=[tsl])
        xf_guard[0] = txf
        for c in range(CN):
            par = c & 1
            tcast = S.op("vector", lambda e, hb=hb, par=par: e.tensor_copy(
                szst[par], srun[:, hb * 128:(hb + 1) * 128]),
                deps=[t_zero_srun, a3.get(("srun", hb)),
                      t_sz.get((hb, c - 2))], drain=False)
            tspl = S.dma("sync", szD[c, hb], szst[par], deps=[tcast])
            t_sz[(hb, c)] = tspl
            dv = decall[:, c * 128 + 2 * hb:c * 128 + 2 * hb + 2]
            S.op("vector", lambda e, hb=hb, dv=dv: e.tensor_tensor(
                srun[:, hb * 128:(hb + 1) * 128],
                srun[:, hb * 128:(hb + 1) * 128],
                dv.unsqueeze(2).broadcast_to([128, 2, 64]), ALU.mult),
                deps=[t_decall], drain=False)
            bA = 6 + (c & 1)
            bB = 6 + ((c + 1) & 1)
            tpx = S.op("tensor", lambda e, c=c, bA=bA: e.transpose(
                PBB[bA][:, 0:128], xf16[:, c * 128:(c + 1) * 128], idb),
                deps=[tsl, bkg[bA]])
            tadds = []
            txss = []
            for hh in range(2):
                h = 2 * hb + hh
                txs = S.op("vector", lambda e, c=c, h=h, hh=hh, bA=bA:
                           e.tensor_scalar(
                               xs2b[:, hh * 64:hh * 64 + 64],
                               PBB[bA][:, hh * 64:hh * 64 + 64],
                               dtdecT[:, c * 128 + h:c * 128 + h + 1], None,
                               ALU.mult),
                           deps=[tpx, t_dtdec, a3.get(("xs2", hh))],
                           drain=False)
                txss.append(txs)
                bank = bB if hh == 0 else bA
                tst = S.op("tensor", lambda e, c=c, hh=hh, bank=bank: e.matmul(
                    PBF[bank][:, 0:64], btmg2[:, c * 128:(c + 1) * 128],
                    xs2b[:, hh * 64:(hh + 1) * 64], start=True, stop=True),
                    deps=[txs, t_btm_ld, bkg[bank]] +
                         ([txss[0], txss[1]] if hh == 1 else []))
                a3[("xs2", hh)] = tst
                ta2 = S.op("vector", lambda e, hb=hb, hh=hh, bank=bank:
                           e.tensor_tensor(
                               srun[:, hb * 128 + hh * 64:hb * 128 + hh * 64 + 64],
                               srun[:, hb * 128 + hh * 64:hb * 128 + hh * 64 + 64],
                               PBF[bank][:, 0:64], ALU.add), deps=[tst],
                           drain=False)
                bkg[bank] = ta2
                a3[("pst", hh)] = ta2
                tadds.append(ta2)
            bkg[bA] = [txss[1], tadds[1]]
            a3[("srun", hb)] = tadds
        a3["btm_rd"] = a3.get("px_rd")
    t_srun = [a3[("srun", hb)] for hb in range(HBLK)]
    A16.pop()
    A32.pop(); A16.pop()
    if "srun" in tap_outs:
        S.dma("sync", tap_outs["srun"], srun, deps=t_srun)
    if phases < 5:
        return _finish_early(t_srun + [xf_guard[0]])

    fenceCC = S.fence()
    A16.push()
    s0st = A16.al(N * P)
    s0b = [A16.al(N * P) for _ in range(2)]
    d1c = A32.al(1); d2c = A32.al(1)
    dful1 = A32.al(128); dful2 = A32.al(128)
    tS0c = S.op("vector", lambda e: e.tensor_copy(s0st, srun),
                deps=t_srun + fenceCC)
    tSo = S.dma("sync", ccSi, s0st, deps=[tS0c])
    tdo = S.dma("sync", ccdi, dcolc, deps=[t_dcol])
    rg = [[0, 1, 2, 3], [4, 5, 6, 7]]
    tcc1 = S.cc(lambda e: e.collective_compute(
        "AllGather", ALU.bypass, replica_groups=rg,
        ins=[ccSi.opt()], outs=[ccSo.opt()]), deps=[tSo])
    tcc2 = S.cc(lambda e: e.collective_compute(
        "AllGather", ALU.bypass, replica_groups=rg,
        ins=[ccdi.opt()], outs=[ccdo.opt()]), deps=[tdo])
    tl0 = S.dma("sync", s0b[0], ccSo[0:128, :], deps=[tcc1])
    tl1 = S.dma("sync", s0b[1], ccSo[128:256, :], deps=[tcc1])
    tld1 = S.dma("sync", d1c, ccdo[128:256, :], deps=[tcc2])
    tld2 = S.dma("sync", d2c, ccdo[256:384, :], deps=[tcc2])
    tfull = []
    for (dcx, dst, mA, mB, tld) in ((d1c, dful1, 0, 1, tld1),
                                    (d2c, dful2, 3, 4, tld2)):
        tpD2 = idmm(dcx, 6, [tld])
        tev = S.op("vector", lambda e, dst=dst, mA=mA, mB=mB: e.tensor_scalar(
            dst, pX[:, 0:128], ccm[:, mA:mA + 1], ccm[:, mB:mB + 1],
            ALU.mult, ALU.add), deps=[tpD2, tc_ccm])
        bkg[6] = tev
        tfull.append(tev)
    S.op("vector", lambda e: e.tensor_copy(srun, s0b[0]),
         deps=[tl0, tS0c] + t_srun)
    S.op("vector", lambda e: e.tensor_tensor(
        srun, srun, dful1.unsqueeze(2).broadcast_to([128, 128, 64]), ALU.mult),
        deps=[tfull[0]])
    tch = S.op("vector", lambda e: e.scalar_tensor_tensor(
        srun, s0b[1], ccm[:, 2:3], srun, ALU.mult, ALU.add), deps=[tl1])
    tl2b = S.dma("sync", s0b[0], ccSo[256:384, :], deps=[tcc1, tch])
    S.op("vector", lambda e: e.tensor_tensor(
        srun, srun, dful2.unsqueeze(2).broadcast_to([128, 128, 64]), ALU.mult),
        deps=[tfull[1]])
    S.op("vector", lambda e: e.scalar_tensor_tensor(
        srun, s0b[0], ccm[:, 5:6], srun, ALU.mult, ALU.add), deps=[tl2b])
    t_sic = S.op("vector", lambda e: e.tensor_scalar(
        srun, srun, ccm[:, 6:7], None, ALU.mult))
    A16.pop()
    if "sincore" in tap_outs:
        S.dma("sync", tap_outs["sincore"], srun, deps=[t_sic])
    if phases < 6:
        return _finish_early([t_sic])

    fenceB = S.fence()
    A32.push(); A16.push()
    lamf = A32.al(128)
    scrB = [A32.al(128) for _ in range(4)]
    scrL = [A32.al(128) for _ in range(4)]
    scrT = [A32.al(1024) for _ in range(2)]
    scrSq = A32.al(128); sqac = A32.al(1)
    sinc = A16.al(N * P)
    szb = A16.al(N * P)
    zc16 = A16.al(HBLK * 128)
    gmtc = A16.al(G * 128)
    cfmc = A16.al(G * 128)
    xtm = [A16.al(1024) for _ in range(2)]
    sgs = [A16.al(1024) for _ in range(2)]
    xdt16 = [A16.al(1024) for _ in range(2)]
    mtb16 = [A16.al(16 * 128) for _ in range(2)]
    ya32 = [A32.al(1024) for _ in range(2)]
    t_ssq0 = S.op("vector", lambda e: e.memset(ssq_sb, 0.0))
    pRb = [pb[0], pb[3], pb[7]]
    pYdb = [pb[1], pb[4]]
    pYob = [pb[2], pb[5]]
    gb = {}
    for _s in range(4):
        for _k in ("pR", "scrB", "scrL"):
            gb[(_k, _s)] = fenceB
    for _p in range(2):
        for _k in ("pYd", "pYo", "xtm", "sgs", "xdt16", "mtb16", "ya32",
                   "scrT"):
            gb[(_k, _p)] = fenceB
    t_zsp = [None] * CN

    def b_group(c, g, tb2, tcf, tgm):
        gp = g & 1
        tx = S.dmaT("scalar", xtm[gp].rearrange("p (b q) -> p b q", b=8),
                    xfD[g * 8:(g + 1) * 8, c], deps=[xf_guard[0],
                                                     gb[("xtm", gp)]]) \
            if False else None
        txs = []
        tgs = []
        for j in range(8):
            txs.append(S.dmaT("scalar", xtm[gp][:, j * 128:(j + 1) * 128],
                              xfD[g * 8 + j, c],
                              deps=[xf_guard[0], gb[("xtm", gp)]]))
            tgs.append(S.dmaT("scalar", sgs[gp][:, j * 128:(j + 1) * 128],
                              sgD[g * 8 + j, c],
                              deps=[t_sg_done, gb[("sgs", gp)]]))
        txd = []
        for j in range(8):
            col = c * 128 + g * 16 + 2 * j
            txd.append(S.op("vector", lambda e, gp=gp, j=j, col=col:
                            e.tensor_tensor(
                                xdt16[gp][:, j * 128:(j + 1) * 128],
                                xtm[gp][:, j * 128:(j + 1) * 128],
                                dtT[:, col:col + 2].unsqueeze(2)
                                .broadcast_to([128, 2, 64]), ALU.mult),
                            deps=[txs[j], gb[("xdt16", gp)]], drain=False))
        tmts = []
        for i in range(16):
            h = g * 16 + i
            col = c * 128 + h
            s3 = i % 3
            s4 = i & 3
            S.op("tensor", lambda e, col=col, s3=s3: e.matmul(
                pRb[s3][:, 0:128],
                cshiT[:, col:col + 1].broadcast_to([128, 128]), idb,
                start=True, stop=False),
                deps=[t_hilo, gb[("pR", s3)], tc_idb])
            tR = S.op("tensor", lambda e, col=col, s3=s3: e.matmul(
                pRb[s3][:, 0:128],
                csloT[:, col:col + 1].broadcast_to([128, 128]), idb,
                start=False, stop=True))
            tcl = S.op("vector", lambda e, col=col, s3=s3, s4=s4:
                       e.tensor_scalar(
                           scrB[s4], pRb[s3][:, 0:128], csT[:, col:col + 1],
                           0.0, ALU.subtract, ALU.min),
                       deps=[tR, t_csT, gb[("scrB", s4)]], drain=False)
            gb[("pR", s3)] = tcl
            tlt = S.op("scalar", lambda e, s4=s4: e.activation(
                scrL[s4], scrB[s4], AF.Exp),
                deps=[tcl, gb[("scrL", s4)]], drain=False)
            gb[("scrB", s4)] = tlt
            tmt = S.op("gpsimd", lambda e, g=g, gp=gp, i=i, s4=s4:
                       e.tensor_tensor(
                           mtb16[gp][:, i * 128:(i + 1) * 128], scrL[s4],
                           gmtc[:, g * 128:(g + 1) * 128], ALU.mult),
                       deps=[tlt, tgm, gb[("mtb16", gp)]], drain=False)
            gb[("scrL", s4)] = tmt
            tmts.append(tmt)
        tyos = []
        for half8 in range(2):
            ybk = half8
            tyo = S.op("tensor", lambda e, c=c, g=g, half8=half8, ybk=ybk:
                       e.matmul(
                           pYob[ybk][:],
                           cfmc[:, g * 128:(g + 1) * 128],
                           sinc[:, (g * 16 + half8 * 8) * 64:
                                (g * 16 + half8 * 8) * 64 + 512],
                           start=True, stop=True),
                       deps=[tb2, tcf, gb[("pYo", ybk)]])
            tyos.append(tyo)
        tt1s = []
        for i in range(16):
            col = c * 128 + g * 16 + i
            ybk = i // 8
            tt1s.append(S.op("scalar", lambda e, i=i, col=col, ybk=ybk, gp=gp:
                             e.activation(
                                 scrT[gp][:, i * 64:(i + 1) * 64],
                                 pYob[ybk][:, (i % 8) * 64:(i % 8) * 64 + 64],
                                 AF.Copy, scale=expcsT[:, col:col + 1]),
                             deps=[tyos[i // 8], t_expcs,
                                   gb[("scrT", gp)]], drain=False))
        gb[("pYo", 0)] = tt1s[7]
        gb[("pYo", 1)] = tt1s[15]
        ty1s = []
        for i in range(16):
            yp = i & 1
            tyd = S.op("tensor", lambda e, gp=gp, i=i, yp=yp: e.matmul(
                pYdb[yp][:, 0:64], mtb16[gp][:, i * 128:(i + 1) * 128],
                xdt16[gp][:, i * 64:(i + 1) * 64], start=True, stop=True),
                deps=[tmts[i], txd[i // 2], gb[("pYd", yp)]])
            h = g * 16 + i
            ty1 = S.op("vector", lambda e, gp=gp, i=i, h=h, yp=yp:
                       e.scalar_tensor_tensor(
                           ya32[gp][:, i * 64:(i + 1) * 64],
                           xtm[gp][:, i * 64:(i + 1) * 64],
                           dful[:, h:h + 1], pYdb[yp][:, 0:64],
                           ALU.mult, ALU.add),
                       deps=[tyd, t_dful, gb[("ya32", gp)]], drain=False)
            gb[("pYd", yp)] = ty1
            ty1s.append(ty1)
        gb[("mtb16", gp)] = ty1s[-1]
        gb[("xdt16", gp)] = ty1s[-1]
        tz_list = []
        tyb_list = []
        for j in range(8):
            tyb = S.op("vector", lambda e, gp=gp, j=j: e.tensor_tensor(
                ya32[gp][:, j * 128:(j + 1) * 128],
                ya32[gp][:, j * 128:(j + 1) * 128],
                scrT[gp][:, j * 128:(j + 1) * 128], ALU.add),
                deps=[tt1s[2 * j], tt1s[2 * j + 1], ty1s[2 * j],
                      ty1s[2 * j + 1]], drain=False)
            tyb_list.append(tyb)
        gb[("scrT", gp)] = tyb_list[-1]
        tsqs = []
        for j in range(8):
            tsq = S.op("scalar", lambda e, gp=gp, j=j: e.activation(
                scrSq, ya32[gp][:, j * 128:(j + 1) * 128], AF.Square,
                accum_out=sqac),
                deps=[tyb_list[j], gb.get("sqac_rd")], drain=False)
            tacc = S.op("vector", lambda e, c=c: e.tensor_tensor(
                ssq_sb[:, c:c + 1], ssq_sb[:, c:c + 1], sqac, ALU.add),
                deps=[tsq, t_ssq0], drain=False)
            gb["sqac_rd"] = tacc
            hb = g * 8 + j
            tz = S.op("vector", lambda e, gp=gp, j=j, hb=hb: e.tensor_tensor(
                zc16[:, hb * 128:(hb + 1) * 128],
                ya32[gp][:, j * 128:(j + 1) * 128],
                sgs[gp][:, j * 128:(j + 1) * 128], ALU.mult),
                deps=[tgs[j], tyb_list[j], gb.get("zc_rd")], drain=False)
            tz_list.append(tz)
        gb[("ya32", gp)] = tz_list[-1]
        gb[("xtm", gp)] = tz_list[-1]
        gb[("sgs", gp)] = tz_list[-1]
        return tz_list, tt1s

    for c in range(CN):
        tlf = idmm(lamc[:, c:c + 1], 6, t_lam)
        tlf2 = S.op("vector", lambda e: e.tensor_copy(lamf, pX[:, 0:128]),
                    deps=[tlf])
        bkg[6] = tlf2
        tgm = S.dma("sync", gmtc, gmtD[c].transpose([1, 0, 2]),
                    deps=[t_gmt_sp, gb.get("gmtc_rd")] + fenceB)
        tcf = S.dma("sync", cfmc, cfmD[c].transpose([1, 0, 2]),
                    deps=[t_cfm_sp, gb.get("cfmc_rd")])
        tsz2 = S.dma("sync", szb, szD[c].transpose([1, 0, 2]),
                     deps=[[t_sz[(hb2, c)] for hb2 in range(HBLK)],
                           gb.get("szb_rd")])
        tb1 = S.op("vector", lambda e: e.tensor_tensor(
            sinc, srun, lamf.unsqueeze(2).broadcast_to([128, 128, 64]),
            ALU.mult), deps=[t_sic, tlf2, gb.get("sinc_rd")])
        tb2 = S.op("vector", lambda e: e.tensor_tensor(
            sinc, sinc, szb, ALU.add), deps=[tsz2])
        gb["szb_rd"] = tb2
        last_tz = None
        last_tt1 = None
        for g in range(G):
            tz_list, tt1s = b_group(c, g, tb2, tcf, tgm)
            last_tz = tz_list
            last_tt1 = tt1s
            gb["gmtc_rd"] = tz_list[-1]
            gb["cfmc_rd"] = tt1s[-1]
            gb["sinc_rd"] = tt1s[-1]
        t_zsp[c] = S.dma("sync", zD[c].transpose([1, 0, 2]), zc16,
                         deps=[gb[("ya32", 0)], gb[("ya32", 1)],
                               gb.get("zc_rd")] +
                              [gb[("xtm", 0)], gb[("xtm", 1)]])
        gb["zc_rd"] = t_zsp[c]
    if "zc0" in tap_outs:
        S.dma("sync", tap_outs["zc0"], zc16, deps=[t_zsp[-1]])
    if "ssq" in tap_outs:
        S.dma("sync", tap_outs["ssq"], ssq_sb, deps=[gb["sqac_rd"]])
    tr1 = S.op("scalar", lambda e: e.activation(
        rstd_sb, ssq_sb, AF.Ln, bias=epsc, scale=1.0 / INTER),
        deps=[gb["sqac_rd"], t_eps])
    t_rstd = S.op("scalar", lambda e: e.activation(
        rstd_sb, rstd_sb, AF.Exp, scale=-0.5), deps=[tr1], hard=[tr1])
    if "rstd" in tap_outs:
        S.dma("sync", tap_outs["rstd"], rstd_sb, deps=[t_rstd])
    A32.pop(); A16.pop()
    if phases < 7:
        return _finish_early([t_zsp, t_rstd])

    fenceC = S.fence()
    A32.push(); A16.push()
    rf = A32.al(TOK)
    fscr = [A32.al(512) for _ in range(2)]
    xrs = [A32.al(128) for _ in range(2)]
    outs = [A32.al(128) for _ in range(2)]
    zhalf = A16.al(HBLK * 512)
    wbC = [[A16.al(32 * 128) for _ in range(2)] for _ in range(2)]
    gbufC = [A16.al(512) for _ in range(2)]
    cg = {}
    for c in range(CN):
        tpr = idmm(rstd_sb[:, c:c + 1], 6, [t_rstd])
        te = S.op("vector", lambda e, c=c: e.tensor_copy(
            rf[:, c * 128:(c + 1) * 128], pX[:, 0:128]), deps=[tpr],
            drain=False)
        bkg[6] = te
        cg["rf"] = te
    t_out_dmas = []
    for half in range(2):
        t_zh = []
        for hb in range(HBLK):
            for cc in range(4):
                c = half * 4 + cc
                t_zh.append(S.dmaT(
                    "scalar",
                    zhalf[:, hb * 512 + cc * 128:hb * 512 + cc * 128 + 128],
                    zD[c, hb], deps=[t_zsp[c], cg.get("zh")] + fenceC))
        cg["zh"] = t_zh
        zh_readers = []
        pend_tail = []

        def emit_tail(tl):
            ob, i, tf2 = tl
            fdone = []
            for cc in range(4):
                c = half * 4 + cc
                parx = cc & 1
                bank = 6 + (cc & 1)
                txr = S.dma("sync", xrs[parx],
                            dx[c * 128:(c + 1) * 128, ob * 128:(ob + 1) * 128],
                            deps=[cg.get(("xrs", parx))])
                tpf = S.op("tensor", lambda e, cc=cc, bank=bank, i=i:
                           e.transpose(
                               PBF[bank][:, 0:128],
                               fscr[i][:, cc * 128:(cc + 1) * 128], id32),
                           deps=[tf2, bkg[bank]])
                tadd = S.op("vector", lambda e, parx=parx, bank=bank:
                            e.tensor_tensor(
                                outs[parx], PBF[bank][:, 0:128],
                                xrs[parx], ALU.add),
                            deps=[tpf, txr, cg.get(("outs", parx))],
                            drain=False)
                bkg[bank] = tadd
                cg[("xrs", parx)] = tadd
                tso = S.dma("sync",
                            dout[c * 128:(c + 1) * 128, ob * 128:(ob + 1) * 128],
                            outs[parx], deps=[tadd])
                cg[("outs", parx)] = tso
                t_out_dmas.append(tso)
                fdone.append(tadd)
            cg[("fscr_rd", i)] = [tf2] + fdone

        for ob in range(32):
            i = ob & 1
            p_ = pb[2 + i]
            tmm = None
            for kk in range(2):
                src2 = dwoT[kk * 4096:(kk + 1) * 4096,
                            ob * 128:(ob + 1) * 128].rearrange(
                    "(k d) o -> d k o", d=128)
                tw2 = S.dma("gpsimd", wbC[i][kk], src2,
                            deps=[cg.get(("wbC", i, kk))] + fenceC)
                cg[("tw", i, kk)] = tw2
            for kk in range(2):
                tw2 = cg[("tw", i, kk)]
                for chb32 in range(32):
                    chb = kk * 32 + chb32
                    st = (chb == 0)
                    sp = (chb == 63)
                    tmm = S.op("tensor",
                               lambda e, i=i, kk=kk, chb=chb, chb32=chb32,
                               st=st, sp=sp, p_=p_: e.matmul(
                                   p_[:],
                                   wbC[i][kk][:, chb32 * 128:(chb32 + 1) * 128],
                                   zhalf[:, chb * 512:chb * 512 + 512],
                                   start=st, stop=sp),
                               deps=[tw2, t_zh, cg.get(("pm", i)), t_rstd])
                cg[("wbC", i, kk)] = tmm
            tgl = S.dma("sync", gbufC[i], gD[ob][:, half * 512:(half + 1) * 512],
                        deps=[t_g_done, cg.get(("gb", i))])
            tf1 = S.op("vector", lambda e, p_=p_, half=half, i=i: e.tensor_tensor(
                fscr[i], p_[:], rf[:, half * 512:(half + 1) * 512], ALU.mult),
                deps=[tmm, cg.get("rf"), cg.get(("fscr_rd", i))], drain=False)
            cg[("pm", i)] = tf1
            tf2 = S.op("gpsimd", lambda e, i=i: e.tensor_tensor(
                fscr[i], fscr[i], gbufC[i], ALU.mult), deps=[tgl, tf1],
                drain=False)
            cg[("gb", i)] = tf2
            pend_tail.append((ob, i, tf2))
            if len(pend_tail) > 1:
                emit_tail(pend_tail.pop(0))
            zh_readers.append(tmm)
        while pend_tail:
            emit_tail(pend_tail.pop(0))
        cg["zh"] = zh_readers
    A32.pop(); A16.pop()
    S.op("sync", lambda e: e.nop(), deps=t_out_dmas)

    S.finalize_and_replay(nc)
    stk.close()
    return nc


def _prep_inputs(inputs):
    f = np.float32
    x = np.ascontiguousarray(np.asarray(inputs["L0_attn_hidden_states"]), dtype=f)
    wpT = np.ascontiguousarray(np.asarray(inputs["L0_attn_in_proj_w"], f).T)
    woT = np.ascontiguousarray(np.asarray(inputs["L0_attn_out_proj_w"], f).T)
    wbT = np.ascontiguousarray(np.asarray(inputs["L0_ffn_W_base"], f).T)
    wrT = np.ascontiguousarray(np.asarray(inputs["L0_ffn_W_router"], f).T)
    laT = np.ascontiguousarray(
        np.asarray(inputs["L0_ffn_lora_A"], f).transpose(1, 0, 2).reshape(DIM, 128))
    lbF = np.ascontiguousarray(np.asarray(inputs["L0_ffn_lora_B"], f).reshape(128, DIM))
    cw = np.ascontiguousarray(np.asarray(inputs["L0_attn_conv_w"], f)[:, 0, :]
                              .reshape(80, 128, 4))
    cbc = np.ascontiguousarray(np.asarray(inputs["L0_attn_conv_b"], f)
                               .reshape(80, 128).T)
    dbc = np.asarray(inputs["L0_attn_dt_bias"], f).reshape(128, 1)
    alc = np.asarray(inputs["L0_attn_A_log"], f).reshape(128, 1)
    dc = np.asarray(inputs["L0_attn_D"], f).reshape(128, 1)
    nwc = np.ascontiguousarray(np.asarray(inputs["L0_attn_norm_w"], f)
                               .reshape(64, 128).T)
    bbc = np.ascontiguousarray(np.asarray(inputs["L0_ffn_b_base"], f)
                               .reshape(32, 128).T)
    brc = np.asarray(inputs["L0_ffn_b_router"], f).reshape(8, 1)
    idb = np.eye(128, dtype=f).astype(ml_dtypes.bfloat16)
    id32 = np.eye(128, dtype=f)
    mut = np.triu(np.ones((128, 128), f), 0)
    invk = np.tile((1.0 / np.arange(1, 9, dtype=f))[None, :], (128, 1))
    common = dict(wpT=wpT, woT=woT, wbT=wbT, wrT=wrT, laT=laT, lbF=lbF, cw=cw,
                  cbc=cbc, dbc=dbc, alc=alc, dc=dc, nwc=nwc, bbc=bbc, brc=brc,
                  idb=idb, id32=id32, mut=mut, invk=invk)
    in_maps = []
    for core in range(8):
        b, q = core // 4, core % 4
        xs = np.ascontiguousarray(x[b, q * TOK:(q + 1) * TOK])
        xh = (np.ascontiguousarray(x[b, q * TOK - 3:q * TOK]) if q > 0
              else np.zeros((3, DIM), f))
        r = q
        m = np.zeros((128, 8), f)
        m[:, 0] = 1.0 if r >= 2 else 0.0
        m[:, 1] = 0.0 if r >= 2 else 1.0
        m[:, 2] = 1.0 if r >= 2 else 0.0
        m[:, 3] = 1.0 if r >= 3 else 0.0
        m[:, 4] = 0.0 if r >= 3 else 1.0
        m[:, 5] = 1.0 if r >= 3 else 0.0
        m[:, 6] = 0.0 if r == 0 else 1.0
        in_maps.append(dict(x=xs, xh=xh, ccm=m, **common))
    return in_maps


def kernel(**inputs):
    in_maps = _prep_inputs(inputs)
    nc = build_core_program(taps=tuple(_TAPS), phases=_PHASES)
    res = run_bass_kernel_spmd(nc, in_maps, core_ids=list(range(8)))
    out = np.zeros((2, 4096, DIM), np.float32)
    for core in range(8):
        b, q = core // 4, core % 4
        if "out" in res.results[core]:
            out[b, q * TOK:(q + 1) * TOK] = res.results[core]["out"]
    if _TAPS:
        kernel._tap_results = res.results
    return out
